# version 18 (final)
# baseline (speedup 1.0000x reference)
import math
import numpy as np
import ml_dtypes
import concourse.bass as bass
import concourse.mybir as mybir
from concourse.bass_utils import run_bass_kernel_spmd
from concourse.bass_core_rust import IndirectOffsetOnAxis

F32 = mybir.dt.float32
BF16 = mybir.dt.bfloat16
I32 = mybir.dt.int32
AF = mybir.ActivationFunctionType
ALU = mybir.AluOpType
AX = mybir.AxisListType
BF = ml_dtypes.bfloat16

CFG_FULL = dict(T=1024, D=2048, HD=128, E=16, I=1408, SI=2816, R=8)


def _CONST_VALS(cfg):
    return [1e-5, -0.5, 1.0 / math.sqrt(cfg["HD"]), 1.0, -1.0,
            1.0 / cfg["D"], 1e30, 0.0, 2.0, 0.5]


class Prog:
    ENGINES = ("tensor", "vector", "scalar", "gpsimd", "sync")

    def __init__(self, nc):
        self.nc = nc
        self.ops = {e: [] for e in self.ENGINES}
        self.cnt = {}
        self.bar_floor = 0

    def sem(self, *names):
        for n in names:
            self.cnt[n] = 0

    def emit(self, eng, fn, wait=(), inc=None, amt=1, nobar=False):
        w = [(s, v) for (s, v) in wait if v and v > 0]
        for s, v in w:
            assert v <= self.cnt[s], f"forward wait {s}:{v}>{self.cnt[s]} on {eng}"
        if not nobar and self.bar_floor:
            w.append(("bar", self.bar_floor))
        tgt = None
        if inc is not None:
            self.cnt[inc] += amt
            tgt = self.cnt[inc]
        self.ops[eng].append((fn, w, inc, amt))
        return tgt

    def dma(self, eng, out, in_, inc, wait=(), accum=False, slow_ok=False):
        def fn(e, o=out, i=in_, a=accum, s=slow_ok):
            kw = {}
            if a:
                kw["accum_op"] = mybir.AluOpType.add
            if s:
                kw["allow_slow_non_contiguous"] = True
            return e.dma_start(out=o, in_=i, **kw)
        w = list(wait)
        w.append((inc, self.cnt[inc]))
        return self.emit(eng, fn, wait=w, inc=inc, amt=16)

    def barrier(self):
        prev = self.bar_floor
        for e in self.ENGINES:
            if e == "sync":
                fn = lambda eng: eng.nop()
            else:
                fn = lambda eng: eng.drain()
            self.emit(e, fn, wait=([("bar", prev)] if prev else []), inc="bar",
                      nobar=True)
        self.bar_floor = self.cnt["bar"]

    def replay(self, block, sems):
        for name in self.ENGINES:
            ops = self.ops[name]

            def mk(ops=ops):
                def body(eng):
                    last = {}
                    for fn, waits, inc, amt in ops:
                        for s, v in waits:
                            if last.get(s, 0) < v:
                                eng.wait_ge(sems[s], v)
                                last[s] = v
                        ins = fn(eng)
                        if inc is not None:
                            ins.then_inc(sems[inc], amt)
                return body

            getattr(block, name)(mk())


def _treeplan(n):
    steps, cur, tmp = [], [("e", i) for i in range(n)], 0
    while len(cur) > 1:
        nxt = []
        for i in range(0, len(cur) - 1, 2):
            dst = ("s", 0) if len(cur) == 2 else ("t", tmp)
            steps.append((dst, cur[i], cur[i + 1]))
            nxt.append(dst)
            tmp += 1
        if len(cur) % 2:
            nxt.append(cur[-1])
        cur = nxt
    return steps


def build_program(cfg, stop_after=None):
    T, D, HD, E, I, SI, R = (cfg[k] for k in ("T", "D", "HD", "E", "I", "SI", "R"))
    H = D // HD
    TT = T // 128
    DT = D // 128
    IC = I // 128
    KT = T // 128
    SH = SI // I
    SHTT = None
    NU = SH + E
    Q5 = min(512, T)
    NQ = T // Q5
    RT = 1.0 / math.sqrt(HD)
    SHTT = TT
    assert SI == SH * I and H * HD == D

    nc = bass.Bass(trn_type="TRN2", target_bir_lowering=False)

    x_in = nc.dram_tensor("x", [T, D], F32, kind="ExternalInput")
    c2d = nc.dram_tensor("c2t", [128, T], F32, kind="ExternalInput")
    s2d = nc.dram_tensor("s2t", [128, T], F32, kind="ExternalInput")
    wq4 = nc.dram_tensor("wq4", [DT, 128, D], F32, kind="ExternalInput")
    wk4 = nc.dram_tensor("wk4", [DT, 128, D], F32, kind="ExternalInput")
    wv4 = nc.dram_tensor("wv4", [DT, 128, D], F32, kind="ExternalInput")
    wotd = nc.dram_tensor("wot", [D, D], F32, kind="ExternalInput")
    gate4 = nc.dram_tensor("gate4", [128, DT * E], F32, kind="ExternalInput")
    w1d = nc.dram_tensor("w1t", [NU, IC, 128, DT * 128], BF16, kind="ExternalInput")
    w3d = nc.dram_tensor("w3t", [NU, IC, 128, DT * 128], BF16, kind="ExternalInput")
    w2d = nc.dram_tensor("w2t", [NU, IC, 128, D], BF16, kind="ExternalInput")
    identd = nc.dram_tensor("ident", [128, 128], BF16, kind="ExternalInput")
    permd = nc.dram_tensor("perm", [128, 128], BF16, kind="ExternalInput")
    onehotd = nc.dram_tensor("onehot", [128, H * H], BF16, kind="ExternalInput")
    ones1d = nc.dram_tensor("ones1", [1, 128], F32, kind="ExternalInput")
    cvalsd = nc.dram_tensor("cvals", [128, 16], F32, kind="ExternalInput")
    out_t = nc.dram_tensor("out", [T, D], F32, kind="ExternalOutput")
    C = min(T, max(2 * (2 * T // E), 128))
    trild = nc.dram_tensor("tril", [128, 128], BF16, kind="ExternalInput")
    tril8d = nc.dram_tensor("tril8", [8, 8], BF16, kind="ExternalInput")
    onescd = nc.dram_tensor("onesc", [128, 1], BF16, kind="ExternalInput")
    iotaed = nc.dram_tensor("iotae", [128, E], F32, kind="ExternalInput")
    bigfd = nc.dram_tensor("bigf", [128, E * C // 128], I32, kind="ExternalInput")
    tokidd = nc.dram_tensor("tokid", [128, TT], I32, kind="ExternalInput")
    ln2_dram = nc.dram_tensor("ln2_dram", [T, D], BF16)
    idxtab = nc.dram_tensor("idxtab", [E * C, 1], I32)
    wtab = nc.dram_tensor("wtab", [E * C, 1], F32)
    ymoeA = nc.dram_tensor("ymoeA", [T, D], F32)
    ymoeB = nc.dram_tensor("ymoeB", [T, D], F32)
    kt_in = nc.dram_tensor("kt_in", [D, T], BF16)
    v_in = nc.dram_tensor("v_in", [T, D], BF16)
    aspace = "Shared" if R > 4 else "Local"
    kt_all = nc.dram_tensor("kt_all", [R * D, T], BF16, addr_space=aspace)
    v_all = nc.dram_tensor("v_all", [R * T, D], BF16, addr_space=aspace)

    p = Prog(nc)
    p.sem("bar", "pe", "act", "dve", "pool",
          "dc", "dx0", "dx1", "dxy", "gw0", "gw1", "gw2", "gwf",
          "dk0", "dk1", "dk2", "dk3", "dv0", "dv1",
          "dko0", "dko1", "dvs0", "dvs1",
          "da0", "da1", "db0", "db1", "dw2",
          "dl0", "dl1", "dit", "dgx", "dix", "dz", "dsa0", "dsa1",
          "dfl", "dout", "cc1", "cc2")

    import contextlib
    st_persist = contextlib.ExitStack()
    SP_ = lambda stk: (lambda *a, **k: stk.enter_context(nc.sbuf_tensor(*a, **k)))
    PP_ = lambda stk: (lambda *a, **k: stk.enter_context(nc.psum_tensor(*a, **k)))
    S0 = SP_(st_persist)

    ident = S0("c_ident", [128, 128], BF16)
    perm = S0("c_perm", [128, 128], BF16)
    onehot = S0("c_onehot", [128, H * H], BF16)
    ones1 = S0("c_ones1", [1, 128], F32)
    gate = S0("c_gate", [128, DT * E], BF16)
    stats = S0("stats", [128, 16], F32)
    sgate = S0("sgate", [128, TT * E], F32)
    dwall = S0("dwall", [128, TT * E], F32)
    tke = S0("tke", [128, 5 * E], F32)
    trils = S0("c_tril", [128, 128], BF16)
    tril8s = S0("c_tril8", [8, 8], BF16)
    onescs = S0("c_onesc", [128, 1], BF16)
    iotaes = S0("c_iotae", [128, E], F32)
    tokids = S0("c_tokid", [128, TT], I32)
    e1all = S0("e1all", [128, TT * E], F32)
    e2all = S0("e2all", [128, TT * E], F32)
    cvals = S0("c_cvals", [128, 16], F32)

    for dst, src in ((ident, identd), (perm, permd), (onehot, onehotd),
                     (trils, trild), (tril8s, tril8d), (onescs, onescd),
                     (iotaes, iotaed), (tokids, tokidd)):
        p.dma("sync", dst[:, :], src[:, :], inc="dc")
    p.dma("sync", ones1[0:1, :], ones1d[0:1, :], inc="dc")
    p.dma("sync", cvals[:, :], cvalsd[:, :], inc="dc")
    CONST = p.cnt["dc"]
    for ci, cv in enumerate(_CONST_VALS(cfg)):
        nc.const_aps.register(F32, cv, cvals[:, ci:ci + 1])
    for _e in Prog.ENGINES:
        p.emit(_e, lambda e: e.nop(), wait=[("dc", CONST)])
    p.dma("gpsimd", gate[:, :], gate4[:, :], inc="gwf")

    st_at = contextlib.ExitStack()
    at = st_at.enter_context(nc.sbuf_tensor("at", [128, H * T], BF16, side="right"))
    st_qt = contextlib.ExitStack()
    qt = st_qt.enter_context(nc.sbuf_tensor("qt", [128, H * T], BF16, side="right"))

    def mk_ln(lnb, dst_t, tpsq, ready, extra_war=None):
        if extra_war is None:
            extra_war = [[], []]
        lnb_war = [0, 0]
        tps_war = [0] * len(tpsq)
        stats_war = [0, 0]

        def emit_tile(tt, src, xw, done_fn=None):
            sl = tt % 2
            so = sl * 8
            v1 = p.emit("vector", lambda e, s=src, o=stats[:, so:so + 1]:
                        e.reduce_sum(o, s, axis=AX.X),
                        wait=xw + [("act", stats_war[sl])], inc="dve")
            a2 = p.emit("scalar", lambda e, s=src, o=lnb[:, sl * D:(sl + 1) * D],
                        a=stats[:, so + 1:so + 2]:
                        e.activation(o, s, AF.Square, accum_out=a),
                        wait=xw + [("pe", lnb_war[sl])] + extra_war[sl], inc="act")
            v = p.emit("vector", lambda e, i=stats[:, so:so + 1], o=stats[:, so + 2:so + 3]:
                       e.tensor_scalar_mul(o, i, 1.0 / D),
                       wait=[("act", a2), ("dve", v1)], inc="dve")
            v = p.emit("vector", lambda e, i=stats[:, so + 1:so + 2], o=stats[:, so + 3:so + 4]:
                       e.tensor_scalar_mul(o, i, 1.0 / D), wait=[("dve", v)], inc="dve")
            v = p.emit("vector", lambda e, m=stats[:, so + 2:so + 3], o=stats[:, so + 4:so + 5]:
                       e.tensor_mul(o, m, m), wait=[("dve", v)], inc="dve")
            vv = p.emit("vector", lambda e, a=stats[:, so + 3:so + 4],
                        b=stats[:, so + 4:so + 5], o=stats[:, so + 5:so + 6]:
                        e.tensor_sub(o, a, b), wait=[("dve", v)], inc="dve")
            al = p.emit("scalar", lambda e, i=stats[:, so + 5:so + 6], o=stats[:, so + 6:so + 7]:
                        e.activation(o, i, AF.Ln, bias=1e-5), wait=[("dve", vv)], inc="act")
            ar = p.emit("scalar", lambda e, i=stats[:, so + 6:so + 7], o=stats[:, so + 7:so + 8]:
                        e.activation(o, i, AF.Exp, scale=-0.5), wait=[("act", al)], inc="act")
            v = p.emit("vector", lambda e, m=stats[:, so + 2:so + 3], r=stats[:, so + 7:so + 8],
                       o=stats[:, so + 4:so + 5]:
                       e.tensor_mul(o, m, r), wait=[("act", ar), ("dve", vv)], inc="dve")
            vnb = p.emit("vector", lambda e, i=stats[:, so + 4:so + 5], o=stats[:, so + 4:so + 5]:
                         e.tensor_scalar_mul(o, i, -1.0), wait=[("dve", v)], inc="dve")
            alnb = p.emit("scalar",
                          lambda e, s=src, o=lnb[:, sl * D:(sl + 1) * D],
                          sc=stats[:, so + 7:so + 8], b=stats[:, so + 4:so + 5]:
                          e.activation(o, s, AF.Identity, scale=sc, bias=b),
                          wait=[("dve", vnb), ("act", ar), ("pe", lnb_war[sl])], inc="act")
            stats_war[sl] = alnb
            ready[("lnb", tt)] = alnb
            if done_fn is not None:
                done_fn(tt, [("act", alnb), ("dve", v1)])
            for dt in range(DT):
                j = (tt * DT + dt) % len(tpsq)
                ttr = p.emit("tensor",
                             lambda e, i=lnb[:, sl * D + dt * 128: sl * D + (dt + 1) * 128],
                             o=tpsq[j][:, :]:
                             e.transpose(o, i, ident[:, :]),
                             wait=[("act", alnb), ("dve", tps_war[j]), ("dc", CONST)],
                             inc="pe")
                if dt == DT - 1:
                    lnb_war[sl] = ttr
                tps_war[j] = p.emit("vector",
                                    lambda e, i=tpsq[j][:, :],
                                    o=dst_t[:, dt * T + tt * 128: dt * T + (tt + 1) * 128]:
                                    e.tensor_copy(o, i), wait=[("pe", ttr)], inc="dve")
                ready[(dt, tt)] = tps_war[j]
        return emit_tile

    st_A = contextlib.ExitStack()
    SA = SP_(st_A)
    zt = SA("c_zt", [128, D], F32)
    ln1t = SA("ln1t", [128, DT * T], BF16)
    xb = SA("xb", [128, 2 * D], F32)
    lnb1 = SA("lnb1", [128, 2 * D], BF16)
    c2 = SA("c_c2", [128, T], F32)
    s2 = SA("c_s2", [128, T], F32)
    p.dma("sync", c2[:, :], c2d[:, :], inc="dc")
    p.dma("sync", s2[:, :], s2d[:, :], inc="dc")
    CS_TGT = p.cnt["dc"]
    st_ps0 = contextlib.ExitStack()
    tps0 = [PP_(st_ps0)("tps0a", [128, 128], BF16),
            PP_(st_ps0)("tps0b", [128, 128], BF16),
            PP_(st_ps0)("tps0c", [128, 128], BF16)]

    p.emit("vector", lambda e: e.memset(zt[:, :], 0.0), inc="dve")
    ZT_T = p.cnt["dve"]
    for tt in range(TT):
        p.dma("sync", ymoeA[tt * 128:(tt + 1) * 128, :], zt[:, :], inc="dz",
              wait=[("dve", ZT_T)])
        p.dma("sync", ymoeB[tt * 128:(tt + 1) * 128, :], zt[:, :], inc="dz",
              wait=[("dve", ZT_T)])
    p.dma("sync", wtab.ap().rearrange("(p n) a -> p (n a)", p=128),
          zt[:, 0:E * C // 128], inc="dz", wait=[("dve", ZT_T)])
    ZMOE = p.cnt["dz"]
    IDXF = p.dma("gpsimd", idxtab.ap().rearrange("(p n) a -> p (n a)", p=128),
                 bigfd[:, :], inc="dit")
    x_tgt = {}
    x_tgt[0] = p.dma("sync", xb[:, 0:D], x_in[0:128, :], inc="dx0")
    if TT > 1:
        x_tgt[1] = p.dma("sync", xb[:, D:2 * D], x_in[128:256, :], inc="dx1")

    def x_done(tt, tgts):
        if tt + 2 < TT:
            x_tgt[tt + 2] = p.dma("sync", xb[:, (tt % 2) * D:(tt % 2 + 1) * D],
                                  x_in[(tt + 2) * 128:(tt + 3) * 128, :],
                                  inc="dx%d" % (tt % 2), wait=tgts)

    ln1_ready = {}
    ln1_tile = mk_ln(lnb1, ln1t, tps0, ln1_ready)
    for tt in range(TT):
        ln1_tile(tt, xb[:, (tt % 2) * D:(tt % 2 + 1) * D],
                 [("dx%d" % (tt % 2), x_tgt[tt])], x_done)
    LN1_ALL = max(ln1_ready.values())
    p.barrier()
    st_ps0.close()

    def early_out(src_sb):
        p.barrier()
        eng = "sync" if src_sb.tensor.dtype == F32 else "gpsimd"
        for tt in range(TT):
            p.dma(eng, out_t[tt * 128:(tt + 1) * 128, :], src_sb,
                  inc="dout", wait=[])
        p.emit("sync", lambda e: e.nop(), wait=[("dout", p.cnt["dout"])])
        with contextlib.ExitStack() as rs:
            sems = {}
            for name in p.cnt.keys():
                sems[name] = rs.enter_context(nc.semaphore("sem_" + name))
            block = rs.enter_context(nc.Block())
            p.replay(block, sems)
        return nc

    if stop_after == "p0":
        return early_out(ln1t[:, 0:D])

    st_P1 = contextlib.ExitStack()
    S1 = SP_(st_P1)
    wb = S1("wb", [128, 3 * D], BF16)
    qe = S1("qe", [128, 3 * T], BF16)
    t1b = S1("t1b", [128, T], F32)
    t2b = S1("t2b", [128, T], F32)
    kst = S1("kst", [128, 2 * T], BF16)
    vst = S1("vst", [128, 2 * T], BF16)
    st_ps1 = contextlib.ExitStack()
    pps = PP_(st_ps1)("pps", [128, 2 * T], F32)
    st_ps1qk = contextlib.ExitStack()
    sps = PP_(st_ps1qk)("sps", [128, 2 * T], F32)

    wb_war = [0, 0, 0]
    pps_war = [[], []]
    sps_war = [[], []]
    qe_war = [[], [], []]
    t12_war = [0]
    kst_war = [0, 0]
    vst_war = [0, 0]
    vtb_war = [0, 0, 0]

    def proj(kind, w4, limit=None):
        for ot in range(DT):
            sl = ot % 2
            sl3 = ot % 3
            ld = p.dma("gpsimd", wb[:, sl3 * D:(sl3 + 1) * D], w4[ot, :, :],
                       inc="gw%d" % sl3, wait=[("pe", wb_war[sl3])])
            mm_stop = None
            for qn in range(NQ):
                for dt in range(DT):
                    w = [("gw%d" % sl3, ld), ("dve", LN1_ALL)]
                    if dt == 0 and qn == 0:
                        w += pps_war[sl]
                    mm_stop = p.emit(
                        "tensor",
                        lambda e, o=pps[:, sl * T + qn * Q5: sl * T + qn * Q5 + Q5],
                        l=wb[:, sl3 * D + dt * 128: sl3 * D + (dt + 1) * 128],
                        r=ln1t[:, dt * T + qn * Q5: dt * T + qn * Q5 + Q5],
                        s0=(dt == 0), s1=(dt == DT - 1):
                        e.matmul(o, l, r, start=s0, stop=s1),
                        wait=w, inc="pe")
            wb_war[sl3] = mm_stop
            pslice = pps[:, sl * T:(sl + 1) * T]
            if kind in ("q", "k"):
                aev = p.emit("scalar", lambda e, i=pslice, o=qe[:, sl3 * T:(sl3 + 1) * T]:
                             e.activation(o, i, AF.Copy),
                             wait=[("pe", mm_stop)] + qe_war[sl3], inc="act")
                if limit == "mm":
                    pps_war[sl] = [("act", aev)]
                    qe_war[sl3] = [("pe", mm_stop)]
                    continue
                sw_stop = None
                for qn in range(NQ):
                    w = [("act", aev)]
                    if qn == 0:
                        w += sps_war[sl]
                    sw_stop = p.emit(
                        "tensor",
                        lambda e, o=sps[:, sl * T + qn * Q5: sl * T + qn * Q5 + Q5],
                        r=qe[:, sl3 * T + qn * Q5: sl3 * T + qn * Q5 + Q5]:
                        e.matmul(o, perm[:, :], r, start=True, stop=True),
                        wait=w, inc="pe")
                qe_war[sl3] = [("pe", sw_stop)]
                if limit == "swap":
                    pps_war[sl] = [("act", aev)]
                    sps_war[sl] = []
                    continue
                v1 = p.emit("vector", lambda e, i=qe[:, sl3 * T:(sl3 + 1) * T]:
                            e.tensor_mul(t1b[:, :], i, c2[:, :]),
                            wait=[("act", aev), ("dve", t12_war[0]),
                                  ("dc", CS_TGT)], inc="dve")
                v2 = p.emit("vector", lambda e, i=sps[:, sl * T:(sl + 1) * T]:
                            e.tensor_mul(t2b[:, :], i, s2[:, :]),
                            wait=[("pe", sw_stop), ("dve", t12_war[0])], inc="dve")
                pps_war[sl] = [("act", aev)]
                sps_war[sl] = [("dve", v2)]
                qe_war[sl3] = [("pe", sw_stop), ("dve", v1)]
                if limit == "rope1":
                    t12_war[0] = v2
                    continue
                if kind == "q":
                    dst = qt[:, ot * T:(ot + 1) * T]
                    vo = p.emit("vector", lambda e, d=dst:
                                e.tensor_add(d, t1b[:, :], t2b[:, :]),
                                wait=[("dve", v1), ("dve", v2)], inc="dve")
                    t12_war[0] = vo
                else:
                    dst = kst[:, sl * T:(sl + 1) * T]
                    vo = p.emit("vector", lambda e, d=dst:
                                e.tensor_add(d, t1b[:, :], t2b[:, :]),
                                wait=[("dve", v1), ("dve", v2),
                                      ("dko%d" % sl, kst_war[sl])], inc="dve")
                    t12_war[0] = vo
                    kst_war[sl] = p.dma("sync", kt_in[ot * 128:(ot + 1) * 128, :],
                                        dst, inc="dko%d" % sl, wait=[("dve", vo)])
            else:
                aev = p.emit("scalar", lambda e, i=pslice, o=qe[:, sl3 * T:(sl3 + 1) * T]:
                             e.activation(o, i, AF.Copy),
                             wait=[("pe", mm_stop)] + qe_war[sl3], inc="act")
                pps_war[sl] = [("act", aev)]
                last_cp = None
                for tc in range(TT):
                    j = (ot * TT + tc) % 3
                    ttr = p.emit("tensor",
                                 lambda e, i=qe[:, sl3 * T + tc * 128: sl3 * T + (tc + 1) * 128],
                                 o=vtp[j][:, :]:
                                 e.transpose(o, i, ident[:, :]),
                                 wait=[("act", aev), ("dve", vtb_war[j])], inc="pe")
                    if tc == TT - 1:
                        qe_war[sl3] = [("pe", ttr)]
                    w = [("pe", ttr)]
                    if tc == 0:
                        w.append(("dvs%d" % sl, vst_war[sl]))
                    last_cp = p.emit("vector",
                                     lambda e, i=vtp[j][:, :],
                                     o=vst[:, sl * T + tc * 128: sl * T + (tc + 1) * 128]:
                                     e.tensor_copy(o, i), wait=w, inc="dve")
                    vtb_war[j] = last_cp
                dst_ap = v_in[:, ot * 128:(ot + 1) * 128].rearrange(
                    "(tc p) hd -> p tc hd", p=128)
                src_ap = vst[:, sl * T:(sl + 1) * T].rearrange(
                    "p (tc hd) -> p tc hd", hd=128)
                vst_war[sl] = p.dma("sync", dst_ap, src_ap,
                                    inc="dvs%d" % sl, wait=[("dve", last_cp)])

    if stop_after in ("p1q_mm", "p1q_swap", "p1q_rope1"):
        proj("q", wq4, limit=stop_after[4:])
        return early_out(qe[:, 0:D])
    proj("q", wq4)
    if stop_after == "p1q":
        return early_out(qt[:, 0:D])
    proj("k", wk4)
    k_wait = [(s, p.cnt[s]) for s in ("dko0", "dko1")]
    cc1t = p.emit("gpsimd", lambda e: e.collective_compute(
        "AllGather", ALU.bypass, replica_groups=[list(range(R))],
        ins=[kt_in.ap().opt()], outs=[kt_all.ap().opt()]),
        wait=k_wait, inc="cc1", amt=1)
    p.barrier()
    st_ps1qk.close()
    st_ps1v = contextlib.ExitStack()
    vtp = [PP_(st_ps1v)("vtpa", [128, 128], BF16),
           PP_(st_ps1v)("vtpb", [128, 128], BF16),
           PP_(st_ps1v)("vtpc", [128, 128], BF16)]
    proj("v", wv4)
    if stop_after == "p1":
        return early_out(qt[:, 0:D])
    v_wait = [(s, p.cnt[s]) for s in ("dvs0", "dvs1")]
    cc2t = p.emit("gpsimd", lambda e: e.collective_compute(
        "AllGather", ALU.bypass, replica_groups=[list(range(R))],
        ins=[v_in.ap().opt()], outs=[v_all.ap().opt()]),
        wait=v_wait, inc="cc2", amt=1)

    if stop_after == "cc":
        return early_out(qt[:, 0:D])
    p.emit("sync", lambda e: e.nop(),
           wait=[("dz", p.cnt["dz"]), ("dko0", p.cnt["dko0"]),
                 ("dko1", p.cnt["dko1"]), ("dvs0", p.cnt["dvs0"]),
                 ("dvs1", p.cnt["dvs1"])])
    p.barrier()
    st_ps1v.close()
    st_ps1.close()
    st_P1.close()
    st_A.close()

    st_P3 = contextlib.ExitStack()
    S3 = SP_(st_P3)
    ktb = S3("ktb", [128, 4 * T], BF16)
    vb = S3("vb", [128, 2 * KT * D], BF16)
    expt = S3("expt", [128, 2 * KT * T], BF16)
    tplan = _treeplan(KT)
    ntmp = sum(1 for d, _, _ in tplan if d[0] == "t")
    tbuf = S3("tbuf", [128, max(1, ntmp) * T], BF16)
    sch = S3("sch", [128, 2 * T], BF16)
    st_ps3 = contextlib.ExitStack()
    scp = PP_(st_ps3)("scp", [128, 2 * T], F32)
    pvp = PP_(st_ps3)("pvp", [128, T], F32)
    psS = PP_(st_ps3)("psS", [H, T], F32)

    ktb_war = [0, 0, 0, 0]
    vb_war = [0, 0]
    kt_tgt = {}
    v_tgt = {}

    def load_kt(m):
        r, h = divmod(m, H)
        kt_tgt[m] = p.dma("sync", ktb[:, (m % 4) * T:(m % 4 + 1) * T],
                          kt_all[r * D + h * 128: r * D + (h + 1) * 128, :],
                          inc="dk%d" % (m % 4),
                          wait=[("pe", ktb_war[m % 4]), ("cc1", 1)])

    def load_v(r):
        src = v_all[r * T:(r + 1) * T, :].rearrange("(kt p) d -> p kt d", p=128)
        dst = vb[:, (r % 2) * KT * D:(r % 2 + 1) * KT * D].rearrange(
            "p (kt d) -> p kt d", d=D)
        v_tgt[r] = p.dma("sync", dst, src,
                         inc="dv%d" % (r % 2),
                         wait=[("pe", vb_war[r % 2]), ("cc2", 1)])

    load_v(0)
    for m0 in range(min(4, R * H)):
        load_kt(m0)

    scp_war = [0, 0]
    pvp_war = []
    expt_war = [[], []]
    sch_war = [0, 0]
    smm_last = 0
    at_last = [0] * H
    pend_dsum = None
    tree_last = [0]
    ndsum = [0]

    def emit_dsum(mm_, hh, wlist):
        nonlocal smm_last
        for qn in range(NQ):
            smm_last = p.emit(
                "tensor",
                lambda e, o=psS[0:H, qn * Q5:(qn + 1) * Q5],
                l=onehot[:, hh * H:(hh + 1) * H],
                r_=sch[:, (mm_ % 2) * T + qn * Q5:(mm_ % 2) * T + qn * Q5 + Q5],
                s0=(ndsum[0] == 0), s1=(ndsum[0] == R * H - 1):
                e.matmul(o, l, r_, start=s0, stop=s1),
                wait=wlist, inc="pe")
        ndsum[0] += 1
        sch_war[mm_ % 2] = smm_last

    for m in range(R * H):
        r, h = divmod(m, H)
        s2_ = m % 2
        if h == 0 and r + 1 < R:
            load_v(r + 1)
        ex_t = {}
        pv_stop = None

        def emit_sc(kt):
            j = kt % 2
            mm = None
            for qn in range(NQ):
                w = [("dk%d" % (m % 4), kt_tgt[m]), ("act", scp_war[j])]
                mm = p.emit("tensor",
                            lambda e, o=scp[:, j * T + qn * Q5: j * T + qn * Q5 + Q5],
                            l=ktb[:, (m % 4) * T + kt * 128:(m % 4) * T + (kt + 1) * 128],
                            r_=qt[:, h * T + qn * Q5: h * T + qn * Q5 + Q5]:
                            e.matmul(o, l, r_, start=True, stop=True),
                            wait=w, inc="pe")
            w2 = [("pe", mm)]
            if kt == 0:
                w2 += expt_war[s2_]
            ex_t[kt] = p.emit(
                "scalar",
                lambda e, i=scp[:, j * T:(j + 1) * T],
                o=expt[:, s2_ * KT * T + kt * T: s2_ * KT * T + (kt + 1) * T]:
                e.activation(o, i, AF.Exp, scale=RT), wait=w2, inc="act")
            scp_war[j] = ex_t[kt]

        def emit_pv(kt):
            nonlocal pv_stop
            for qn in range(NQ):
                w = [("act", ex_t[kt]), ("dv%d" % (r % 2), v_tgt[r])]
                if kt == 0 and qn == 0:
                    w += pvp_war
                pv_stop = p.emit(
                    "tensor",
                    lambda e, o=pvp[:, qn * Q5: qn * Q5 + Q5],
                    l=vb[:, (r % 2) * KT * D + kt * D + h * 128:
                          (r % 2) * KT * D + kt * D + (h + 1) * 128],
                    r_=expt[:, s2_ * KT * T + kt * T + qn * Q5:
                            s2_ * KT * T + kt * T + qn * Q5 + Q5],
                    s0=(kt == 0), s1=(kt == KT - 1):
                    e.matmul(o, l, r_, start=s0, stop=s1),
                    wait=w, inc="pe")

        emit_sc(0)
        if KT > 1:
            emit_sc(1)
        for kt in range(KT):
            if kt + 2 < KT:
                emit_sc(kt + 2)
            emit_pv(kt)
        if pend_dsum is not None:
            emit_dsum(*pend_dsum)
            pend_dsum = None
        ktb_war[m % 4] = pv_stop
        if h == H - 1:
            vb_war[r % 2] = pv_stop
        node_t = {}
        lastv = None
        lastp = 0
        first_step = True
        for si, (dst, a, b) in enumerate(tplan):
            def ap_of(n):
                k, i = n
                if k == "e":
                    return expt[:, s2_ * KT * T + i * T:(s2_ * KT * T + (i + 1) * T)]
                if k == "t":
                    return tbuf[:, i * T:(i + 1) * T]
                return sch[:, (m % 2) * T:(m % 2 + 1) * T]
            eng = "vector"
            w = []
            for n in (a, b):
                if n[0] == "e":
                    w.append(("act", ex_t[n[1]]))
                elif n in node_t:
                    w.append(node_t[n])
            if dst[0] == "s":
                w.append(("pe", sch_war[m % 2]))
            if first_step:
                w.append(("dve", tree_last[0]))
                first_step = False
            tgt = p.emit(eng, lambda e, o=ap_of(dst), x=ap_of(a), y=ap_of(b):
                         e.tensor_add(o, x, y), wait=w, inc="dve")
            node_t[dst] = ("dve", tgt)
            lastv = tgt
            tree_last[0] = tgt
        pend_dsum = (m, h, [node_t[("s", 0)]])
        ew = [("pe", pv_stop)]
        if lastv:
            ew.append(("dve", lastv))
        expt_war[s2_] = ew
        if r == 0:
            vac = p.emit("vector", lambda e, i=pvp[:, :],
                         o=at[:, h * T:(h + 1) * T]:
                         e.tensor_copy(o, i), wait=[("pe", pv_stop)], inc="dve")
        else:
            vac = p.emit("vector", lambda e, i=pvp[:, :],
                         o=at[:, h * T:(h + 1) * T]:
                         e.tensor_add(o, o, i),
                         wait=[("pe", pv_stop), ("dve", at_last[h])], inc="dve")
        at_last[h] = vac
        pvp_war = [("dve", vac)]
        if m + 4 < R * H:
            load_kt(m + 4)

    if stop_after == "p3":
        return early_out(at[:, 0:D])

    if pend_dsum is not None:
        emit_dsum(*pend_dsum)

    p.barrier()
    st_P3.close()
    st_P4 = contextlib.ExitStack()
    S4 = SP_(st_P4)
    sums = S4("sums", [H, T], F32)
    rsums = S4("rsums", [H, T], F32)
    rsflat = S4("rsflat", [1, H * T], F32)
    vsc = p.emit("vector", lambda e: e.tensor_copy(sums[0:H, :], psS[0:H, :]),
                 wait=[("pe", smm_last)], inc="dve")
    vrc = p.emit("vector", lambda e: e.reciprocal(rsums[0:H, :], sums[0:H, :]),
                 wait=[("dve", vsc)], inc="dve")
    p.barrier()
    st_ps3.close()
    st_ps4 = contextlib.ExitStack()
    rps = PP_(st_ps4)("rps", [128, 2 * T], F32)
    at_sc_h = {}
    fl = p.dma("sync", rsflat[0:1, :].rearrange("a (h t) -> a h t", h=H),
               rsums[0:H, :],
               inc="dfl", wait=[("dve", vrc)])
    rps_war = [0, 0]
    for h in range(H):
        r_stop = None
        for qn in range(NQ):
            w = [("dfl", fl)]
            if qn == 0:
                w.append(("dve", rps_war[h % 2]))
            r_stop = p.emit("tensor",
                            lambda e, o=rps[:, (h % 2) * T + qn * Q5:
                                            (h % 2) * T + qn * Q5 + Q5],
                            r_=rsflat[0:1, h * T + qn * Q5: h * T + qn * Q5 + Q5]:
                            e.matmul(o, ones1[0:1, :], r_, start=True, stop=True),
                            wait=w, inc="pe")
        rps_war[h % 2] = p.emit("vector",
                                lambda e, o=at[:, h * T:(h + 1) * T],
                                i=rps[:, (h % 2) * T:(h % 2 + 1) * T]:
                                e.tensor_mul(o, o, i),
                                wait=[("pe", r_stop), ("dve", at_last[h])], inc="dve")
        at_last[h] = rps_war[h % 2]
        at_sc_h[h] = rps_war[h % 2]
    AT_SCALED = p.cnt["dve"]
    p.emit("sync", lambda e: e.nop(), wait=[("dfl", p.cnt["dfl"])])
    p.barrier()
    st_ps4.close()
    st_P4.close()
    st_qt.close()

    if stop_after == "p4":
        return early_out(at[:, 0:D])

    st_y1 = contextlib.ExitStack()
    y1 = SP_(st_y1)("y1", [128, TT * D], F32)
    ln2t = SP_(st_y1)("ln2t", [128, DT * T], BF16)
    for tt in range(TT):
        p.dma("scalar", y1[:, tt * D:(tt + 1) * D],
              x_in[tt * 128:(tt + 1) * 128, :], inc="dxy")
    XY_ALL = p.cnt["dxy"]
    st_P5 = contextlib.ExitStack()
    wo_sb = SP_(st_P5)("wo_sb", [128, DT * D], BF16)
    lnb2 = SP_(st_P5)("lnb2", [128, 2 * D], BF16)
    OH = 2 if D >= 1024 else 1
    OW = D // OH
    O5 = min(512, OW)
    NO5 = OW // O5
    st_ps5 = contextlib.ExitStack()
    wop = PP_(st_ps5)("wop", [128, 2 * OW], F32)
    tps6 = [PP_(st_ps5)("tps6a", [128, 128], BF16),
            PP_(st_ps5)("tps6b", [128, 128], BF16)]
    gps = [PP_(st_ps5)("gpsa", [128, E], F32),
           PP_(st_ps5)("gpsb", [128, E], F32)]

    for oh in range(OH):
        for dt in range(DT):
            p.dma("gpsimd", wo_sb[:, dt * D + oh * OW: dt * D + (oh + 1) * OW],
                  wotd[dt * 128:(dt + 1) * 128, oh * OW:(oh + 1) * OW], inc="gwf")
    WO_ALL = p.cnt["gwf"]
    y1_attn = {}
    wop_war = [0, 0]
    ln2_ready = {}
    ln2_xwar = [[], []]
    ln2_tile = mk_ln(lnb2, ln2t, tps6, ln2_ready, extra_war=ln2_xwar)
    gps_war = [0, 0]
    dw_tgt = {}
    tke_war = [0]

    def wo_tile(tt):
        for oh in range(OH):
            sl = (tt * OH + oh) % 2
            mm_stop = None
            for n5 in range(NO5):
                for dt in range(DT):
                    w = [("gwf", WO_ALL), ("dve", at_sc_h[dt])]
                    if dt == 0 and n5 == 0:
                        w += [("dve", wop_war[sl])]
                    mm_stop = p.emit(
                        "tensor",
                        lambda e, o=wop[:, sl * OW + n5 * O5: sl * OW + n5 * O5 + O5],
                        l=at[:, dt * T + tt * 128: dt * T + (tt + 1) * 128],
                        r_=wo_sb[:, dt * D + oh * OW + n5 * O5:
                                 dt * D + oh * OW + n5 * O5 + O5],
                        s0=(dt == 0), s1=(dt == DT - 1):
                        e.matmul(o, l, r_, start=s0, stop=s1),
                        wait=w, inc="pe")
            wop_war[sl] = p.emit(
                "vector",
                lambda e, o=y1[:, tt * D + oh * OW: tt * D + (oh + 1) * OW],
                i=wop[:, sl * OW:(sl + 1) * OW]:
                e.tensor_add(o, o, i),
                wait=[("pe", mm_stop), ("dxy", XY_ALL)], inc="dve")
        y1_attn[tt] = wop_war[(tt * OH + OH - 1) % 2]

    def gate_tile(tt):
        sl = tt % 2
        g_stop = None
        for dt in range(DT):
            w = [("dve", ln2_ready[(dt, tt)]), ("gwf", p.cnt["gwf"])]
            if dt == 0:
                w.append(("act", gps_war[sl]))
            g_stop = p.emit("tensor",
                            lambda e, o=gps[sl][:, :],
                            l=ln2t[:, dt * T + tt * 128: dt * T + (tt + 1) * 128],
                            r_=gate[:, dt * E:(dt + 1) * E],
                            s0=(dt == 0), s1=(dt == DT - 1):
                            e.matmul(o, l, r_, start=s0, stop=s1),
                            wait=w, inc="pe")
        asg = p.emit("scalar", lambda e, i=gps[sl][:, :],
                     o=sgate[:, tt * E:(tt + 1) * E]:
                     e.activation(o, i, AF.Sigmoid),
                     wait=[("pe", g_stop)], inc="act")
        gps_war[sl] = asg
        sg = sgate[:, tt * E:(tt + 1) * E]
        v = p.emit("vector", lambda e, i=sg: e.reduce_max(tke[:, 4 * E:4 * E + 1], i, axis=AX.X),
                   wait=[("act", asg), ("dve", tke_war[0])], inc="dve")
        v = p.emit("vector", lambda e, i=sg: e.tensor_scalar(
            tke[:, 0:E], i, tke[:, 4 * E:4 * E + 1], None, ALU.is_ge),
            wait=[("dve", v)], inc="dve")
        v = p.emit("vector", lambda e: e.tensor_scalar(
            tke[:, E:2 * E], tke[:, 0:E], 1e30, None, ALU.mult),
            wait=[("dve", v)], inc="dve")
        v = p.emit("vector", lambda e, i=sg: e.tensor_sub(
            tke[:, 2 * E:3 * E], i, tke[:, E:2 * E]), wait=[("dve", v)], inc="dve")
        v = p.emit("vector", lambda e: e.reduce_max(
            tke[:, 4 * E + 1:4 * E + 2], tke[:, 2 * E:3 * E], axis=AX.X),
            wait=[("dve", v)], inc="dve")
        v = p.emit("vector", lambda e: e.tensor_scalar(
            tke[:, E:2 * E], tke[:, 2 * E:3 * E], tke[:, 4 * E + 1:4 * E + 2],
            None, ALU.is_ge), wait=[("dve", v)], inc="dve")
        v = p.emit("vector", lambda e, o=e1all[:, tt * E:(tt + 1) * E]:
                   e.tensor_copy(o, tke[:, 0:E]), wait=[("dve", v)], inc="dve")
        v = p.emit("vector", lambda e, o=e2all[:, tt * E:(tt + 1) * E]:
                   e.tensor_copy(o, tke[:, E:2 * E]), wait=[("dve", v)], inc="dve")
        v = p.emit("vector", lambda e: e.tensor_add(
            tke[:, 3 * E:4 * E], tke[:, 0:E], tke[:, E:2 * E]),
            wait=[("dve", v)], inc="dve")
        v = p.emit("vector", lambda e, i=sg: e.tensor_mul(
            tke[:, 0:E], tke[:, 3 * E:4 * E], i), wait=[("dve", v)], inc="dve")
        v = p.emit("vector", lambda e: e.tensor_add(
            tke[:, 4 * E + 2:4 * E + 3], tke[:, 4 * E:4 * E + 1],
            tke[:, 4 * E + 1:4 * E + 2]), wait=[("dve", v)], inc="dve")
        v = p.emit("vector", lambda e: e.reciprocal(
            tke[:, 4 * E + 3:4 * E + 4], tke[:, 4 * E + 2:4 * E + 3]),
            wait=[("dve", v)], inc="dve")
        dw_tgt[tt] = p.emit("vector", lambda e, o=dwall[:, tt * E:(tt + 1) * E]:
                            e.tensor_scalar(o, tke[:, 0:E], tke[:, 4 * E + 3:4 * E + 4],
                                            None, ALU.mult), wait=[("dve", v)], inc="dve")
        tke_war[0] = dw_tgt[tt]

    for tt in range(TT):
        wo_tile(tt)
        ln2_tile(tt, y1[:, tt * D:(tt + 1) * D], [("dve", y1_attn[tt])])
        sl_ = tt % 2
        dl = p.dma("sync", ln2_dram[tt * 128:(tt + 1) * 128, :],
                   lnb2[:, sl_ * D:(sl_ + 1) * D], inc="dl%d" % sl_,
                   wait=[("act", ln2_ready[("lnb", tt)])])
        ln2_xwar[sl_][:] = [("dl%d" % sl_, dl)]
        gate_tile(tt)
    LN2D = [("dl0", p.cnt["dl0"]), ("dl1", p.cnt["dl1"])]

    p.emit("sync", lambda e: e.nop(),
           wait=[("dl0", p.cnt["dl0"]), ("dl1", p.cnt["dl1"])])
    p.barrier()
    st_ps5.close()
    st_P5.close()
    st_at.close()

    if stop_after == "p5" or stop_after == "p6":
        return early_out(y1[:, 0:D])

    st_P7 = contextlib.ExitStack()
    S7 = SP_(st_P7)
    rbT = nc.gpsimd.alloc_register("rbT")
    rbEC = nc.gpsimd.alloc_register("rbEC")
    p.emit("gpsimd", lambda e: e.reg_mov(rbT, T - 1))
    p.emit("gpsimd", lambda e: e.reg_mov(rbEC, E * C - 1))
    ht = S7("ht", [128, IC * T], BF16)
    s1a = S7("s1a", [128, 2 * D], BF16)
    s1b = S7("s1b", [128, 2 * D], BF16)
    w2sb = S7("w2sb", [128, IC * D], BF16)
    silu = S7("silu", [128, D], F32)
    ysc = silu

    ixb = S7("ixb", [128, 2 * max(2, C // 128)], I32)
    wtb = S7("wtb", [128, 2 * max(2, C // 128)], F32)
    pfsb = S7("pfsb", [128, TT * E], F32)
    possb = pfsb
    cntsb = S7("cntsb", [128, TT * E], BF16)
    mkb = S7("mkb", [128, TT * E], BF16)
    basef = S7("basef", [128, 64 + TT * E], F32)
    ixw = S7("ixw", [128, 4], I32)
    wvw = S7("wvw", [128, 4], F32)
    OW2 = min(1024, D)
    NO2 = D // OW2
    W5 = min(512, OW2)
    N5 = OW2 // W5
    st_ps7y = contextlib.ExitStack()
    yps = PP_(st_ps7y)("yps", [128, 2 * OW2], F32)
    st_ps7a = contextlib.ExitStack()
    aps = PP_(st_ps7a)("aps", [128, T], F32)
    bps = PP_(st_ps7a)("bps", [128, T], F32)

    s1a_war = [0, 0]
    s1b_war = [0, 0]
    aps_war = 0
    bps_war = 0
    ht_war = {}
    ht_tgt = {}
    w2_tgt = {}
    wy_war = [None, None]
    y_final = dict((tt, 0) for tt in range(TT))
    stage2_last_mm = 0

    def w2loads(u, wait_pe):
        for ic in range(IC):
            p.dma("gpsimd", w2sb[:, ic * D:(ic + 1) * D],
                  w2d[u, ic, :, :], inc="dw2",
                  wait=([("pe", wait_pe)] if ic == 0 else []))
        w2_tgt[u] = p.cnt["dw2"]

    def stage1_chunk(u, ic):
        nonlocal aps_war, bps_war
        sl = ic % 2
        la = p.dma("sync", s1a[:, sl * D:(sl + 1) * D], w1d[u, ic, :, :],
                   inc="da%d" % sl, wait=[("pe", s1a_war[sl])])
        lb = p.dma("scalar", s1b[:, sl * D:(sl + 1) * D], w3d[u, ic, :, :],
                   inc="db%d" % sl, wait=[("pe", s1b_war[sl])])
        a_stop = b_stop = None
        for qn in range(NQ):
            for dt in range(DT):
                w = [("da%d" % sl, la), ("dve", ln2_ready[(dt, TT - 1)])]
                if dt == 0 and qn == 0:
                    w.append(("act", aps_war))
                a_stop = p.emit(
                    "tensor",
                    lambda e, o=aps[:, qn * Q5: qn * Q5 + Q5],
                    l=s1a[:, sl * D + dt * 128: sl * D + (dt + 1) * 128],
                    r_=ln2t[:, dt * T + qn * Q5: dt * T + qn * Q5 + Q5],
                    s0=(dt == 0), s1=(dt == DT - 1):
                    e.matmul(o, l, r_, start=s0, stop=s1), wait=w, inc="pe")
        s1a_war[sl] = a_stop
        for qn in range(NQ):
            for dt in range(DT):
                w = [("db%d" % sl, lb)]
                if dt == 0 and qn == 0:
                    w.append(("dve", bps_war))
                b_stop = p.emit(
                    "tensor",
                    lambda e, o=bps[:, qn * Q5: qn * Q5 + Q5],
                    l=s1b[:, sl * D + dt * 128: sl * D + (dt + 1) * 128],
                    r_=ln2t[:, dt * T + qn * Q5: dt * T + qn * Q5 + Q5],
                    s0=(dt == 0), s1=(dt == DT - 1):
                    e.matmul(o, l, r_, start=s0, stop=s1), wait=w, inc="pe")
        s1b_war[sl] = b_stop
        aps_war = p.emit("scalar", lambda e: e.activation(silu[:, 0:T], aps[:, :], AF.Silu),
                         wait=[("pe", a_stop), ("dve", bps_war)], inc="act")
        bps_war = p.emit("vector", lambda e, o=ht[:, ic * T:(ic + 1) * T]:
                         e.tensor_mul(o, silu[:, 0:T], bps[:, :]),
                         wait=[("act", aps_war), ("pe", b_stop),
                               ("pe", ht_war.get(u - 1, 0))], inc="dve")
        ht_tgt[(u, ic)] = bps_war

    def stage2_chunk(u, tt, oh, sl):
        nonlocal stage2_last_mm
        mm_stop = None
        for n5 in range(N5):
            for ic in range(IC):
                w = [("dve", ht_tgt[(u, ic)]), ("dw2", w2_tgt[u])]
                if ic == 0 and n5 == 0 and wy_war[sl] is not None:
                    w.append(wy_war[sl])
                mm_stop = p.emit(
                    "tensor",
                    lambda e, o=yps[:, sl * OW2 + n5 * W5: sl * OW2 + n5 * W5 + W5],
                    l=ht[:, ic * T + tt * 128: ic * T + (tt + 1) * 128],
                    r_=w2sb[:, ic * D + oh * OW2 + n5 * W5:
                            ic * D + oh * OW2 + n5 * W5 + W5],
                    s0=(ic == 0), s1=(ic == IC - 1):
                    e.matmul(o, l, r_, start=s0, stop=s1), wait=w, inc="pe")
        stage2_last_mm = mm_stop
        ysl = y1[:, tt * D + oh * OW2: tt * D + (oh + 1) * OW2]
        t_ = p.emit("vector",
                    lambda e, o=ysl, i=yps[:, sl * OW2:(sl + 1) * OW2]:
                    e.tensor_add(o, o, i),
                    wait=[("pe", mm_stop), ("dve", y_final[tt])], inc="dve")
        wy_war[sl] = ("dve", t_)
        y_final[tt] = max(y_final[tt], t_)

    w2loads(0, 0)
    for ic in range(IC):
        stage1_chunk(0, ic)

    pA = yps[:, 0:128]
    pB = yps[:, OW2:OW2 + 128]
    NE = TT * E
    MSK = ("dve", dw_tgt[TT - 1])
    v = p.emit("vector", lambda e: e.tensor_add(mkb[:, :], e1all[:, :], e2all[:, :]),
               wait=[MSK], inc="dve")
    mpf = p.emit("tensor", lambda e: e.matmul(pA[:, 0:NE], trils[:, :], mkb[:, 0:NE],
                                              start=True, stop=True),
                 wait=[("dve", v), ("dc", CONST)], inc="pe")
    mcn = p.emit("tensor", lambda e: e.matmul(pB[0:1, 0:NE], onescs[:, 0:1], mkb[:, 0:NE],
                                              start=True, stop=True),
                 wait=[("dve", v)], inc="pe")
    vpf = p.emit("vector", lambda e: e.tensor_copy(pfsb[:, :], pA[:, 0:NE]),
                 wait=[("pe", mpf)], inc="dve")
    acn = p.emit("scalar", lambda e: e.activation(cntsb[0:1, 0:NE], pB[0:1, 0:NE], AF.Copy),
                 wait=[("pe", mcn)], inc="act")
    fl1 = p.dma("sync", mkb[0:TT, 0:E],
                cntsb[0:1, 0:NE].rearrange("a (t e) -> a t e", t=TT),
                inc="dfl", wait=[("act", acn), ("pe", mpf), ("pe", mcn)])
    mbs = p.emit("tensor", lambda e: e.matmul(pB[0:SHTT, 0:E], tril8s[0:SHTT, 0:SHTT],
                                              mkb[0:SHTT, 0:E],
                                              start=True, stop=True),
                 wait=[("dfl", fl1), ("dve", vpf)], inc="pe")
    abs_ = p.emit("scalar", lambda e: e.activation(basef[0:SHTT, 0:E], pB[0:SHTT, 0:E],
                                                   AF.Copy),
                  wait=[("pe", mbs)], inc="act")
    fl2 = p.dma("sync", basef[0:1, 64:64 + NE].rearrange("a (t e) -> a t e", t=TT),
                basef[0:SHTT, 0:E],
                inc="dfl", wait=[("act", abs_)])
    mbb = p.emit("tensor", lambda e: e.matmul(pA[:, 0:NE], ones1[0:1, :],
                                              basef[0:1, 64:64 + NE],
                                              start=True, stop=True),
                 wait=[("dfl", fl2), ("dve", vpf)], inc="pe")
    v = p.emit("vector", lambda e: e.tensor_add(pfsb[:, :], pfsb[:, :], pA[:, 0:NE]),
               wait=[("pe", mbb), ("dve", vpf)], inc="dve")
    vpos = p.emit("vector", lambda e: e.tensor_scalar(pfsb[:, :], pfsb[:, :], -1.0,
                                                      None, ALU.add),
                  wait=[("dve", v)], inc="dve")
    wy_war[0] = ("dve", vpos)
    wy_war[1] = ("dve", vpos)

    sc_prev = [IDXF, IDXF, IDXF, IDXF]
    for tt in range(TT):
        for k in (1, 2):
            msk = (e1all if k == 1 else e2all)[:, tt * E:(tt + 1) * E]
            slk = (tt * 2 + (k - 1)) % 4
            v = p.emit("vector", lambda e, m=msk, ps=possb[:, tt * E:(tt + 1) * E]:
                       e.tensor_mul(tke[:, 0:E], ps, m),
                       wait=[("dve", vpos), ("dit", sc_prev[slk])], inc="dve")
            v = p.emit("vector", lambda e: e.reduce_sum(tke[:, 4 * E:4 * E + 1],
                                                        tke[:, 0:E], axis=AX.X),
                       wait=[("dve", v)], inc="dve")
            v = p.emit("vector", lambda e, m=msk: e.tensor_mul(tke[:, E:2 * E],
                                                               iotaes[:, :], m),
                       wait=[("dve", v)], inc="dve")
            v = p.emit("vector", lambda e: e.reduce_sum(tke[:, 4 * E + 1:4 * E + 2],
                                                        tke[:, E:2 * E], axis=AX.X),
                       wait=[("dve", v)], inc="dve")
            v = p.emit("vector", lambda e: e.tensor_scalar(
                tke[:, 4 * E + 2:4 * E + 3], tke[:, 4 * E + 1:4 * E + 2], float(C),
                None, ALU.mult), wait=[("dve", v)], inc="dve")
            v = p.emit("vector", lambda e: e.tensor_add(
                tke[:, 4 * E + 2:4 * E + 3], tke[:, 4 * E + 2:4 * E + 3],
                tke[:, 4 * E:4 * E + 1]), wait=[("dve", v)], inc="dve")
            v = p.emit("vector", lambda e, o=ixw[:, slk:slk + 1]:
                       e.tensor_copy(o, tke[:, 4 * E + 2:4 * E + 3]),
                       wait=[("dve", v)], inc="dve")
            v = p.emit("vector", lambda e, m=msk, dw=dwall[:, tt * E:(tt + 1) * E]:
                       e.tensor_mul(tke[:, 2 * E:3 * E], dw, m),
                       wait=[("dve", v)], inc="dve")
            vw = p.emit("vector", lambda e, o=wvw[:, slk:slk + 1]:
                        e.reduce_sum(o, tke[:, 2 * E:3 * E], axis=AX.X),
                        wait=[("dve", v)], inc="dve")
            s1_ = p.emit("gpsimd", lambda e, ix=ixw[:, slk:slk + 1], s=tokids[:, tt:tt + 1]:
                         e.indirect_dma_start(
                             idxtab[:, :], IndirectOffsetOnAxis(ix, 0), s, None,
                             bounds_check=rbEC, oob_is_err=False),
                         wait=[("dve", vw), ("dc", CONST),
                               ("dit", p.cnt["dit"])],
                         inc="dit", amt=16)
            sc_prev[slk] = p.emit("gpsimd", lambda e, ix=ixw[:, slk:slk + 1],
                                  s=wvw[:, slk:slk + 1]:
                                  e.indirect_dma_start(
                                      wtab[:, :], IndirectOffsetOnAxis(ix, 0), s, None,
                                      bounds_check=rbEC, oob_is_err=False),
                                  wait=[("dz", ZMOE), ("dit", p.cnt["dit"])],
                                  inc="dit", amt=16)
    IDX_DONE = p.cnt["dit"]

    sl = 0
    for u in range(SH):
        if u > 0:
            for ic in range(IC):
                stage1_chunk(u, ic)
        for tt in range(TT):
            for oh in range(NO2):
                stage2_chunk(u, tt, oh, sl % 2)
                sl += 1
        ht_war[u] = stage2_last_mm
        w2loads(u + 1, stage2_last_mm)

    if stop_after == "p7s":
        return early_out(y1[:, 0:D])

    p.barrier()
    st_ps7a.close()
    st_ps7e = contextlib.ExitStack()
    apse = PP_(st_ps7e)("apse", [128, C], F32)
    bpse = PP_(st_ps7e)("bpse", [128, C], F32)
    tpx = [PP_(st_ps7e)("tpxa", [128, 512], BF16),
           PP_(st_ps7e)("tpxb", [128, 512], BF16)]
    TG = 4 if DT % 4 == 0 else 1
    NCT = C // 128
    HTC = IC * C
    if HTC + DT * C + 2 * D <= IC * T:
        xgT = ht[:, HTC: HTC + DT * C]
        xg_slots = [ht[:, HTC + DT * C + s * D: HTC + DT * C + (s + 1) * D]
                    for s in range(2)]
    else:
        _xgt = S7("xgt_sb", [128, DT * C], BF16)
        _xgs = S7("xg_sb", [128, 2 * D], BF16)
        xgT = _xgt[:, :]
        xg_slots = [_xgs[:, s * D:(s + 1) * D] for s in range(2)]
    CQ = min(512, C)
    NCQ = C // CQ
    eht_war = stage2_last_mm
    exgT_war = [stage2_last_mm, stage2_last_mm]

    easp_war = 0
    ebsp_war = 0
    eysc_war = [0, 0]
    tpx_war = [0, 0]
    ixb_war = [0, 0]
    est2_last = stage2_last_mm

    xgt_rdy = {}

    def fetch(e_):
        par = e_ % 2
        p.dma("sync", ixb[:, par * NCT:(par + 1) * NCT],
              idxtab[e_ * C:(e_ + 1) * C, 0:1].rearrange("(ct p) a -> p (ct a)", p=128),
              inc="dix", wait=[("dit", IDX_DONE), ("dsa%d" % par, ixb_war[par])],
              slow_ok=True)
        p.dma("sync", wtb[:, par * NCT:(par + 1) * NCT],
              wtab[e_ * C:(e_ + 1) * C, 0:1].rearrange("(ct p) a -> p (ct a)", p=128),
              inc="dix", slow_ok=True)
        DIX = p.cnt["dix"]
        g_tgt = {}
        for ct in range(NCT):
            g_tgt[ct] = p.emit(
                "gpsimd",
                lambda e, ix=ixb[:, par * NCT + ct: par * NCT + ct + 1],
                o=xg_slots[ct % 2]:
                e.indirect_dma_start(o, None, ln2_dram[:, :],
                                     IndirectOffsetOnAxis(ix, 0),
                                     bounds_check=rbT, oob_is_err=False),
                wait=[("dix", DIX), ("pe", exgT_war[ct % 2]),
                      ("dgx", p.cnt["dgx"])] + LN2D,
                inc="dgx", amt=16)
        for ct in range(NCT):
            for dt0 in range(0, DT, TG):
                j = ((ct * DT + dt0) // TG) % 2
                ttr = None
                for q in range(TG):
                    dt = dt0 + q
                    ttr = p.emit("tensor",
                                 lambda e, i=xg_slots[ct % 2][:, dt * 128:(dt + 1) * 128],
                                 o=tpx[j][:, q * 128:(q + 1) * 128]:
                                 e.transpose(o, i, ident[:, :]),
                                 wait=[("dgx", g_tgt[ct]), ("dve", tpx_war[j]),
                                       ("pe", est2_last)], inc="pe")
                if dt0 + TG == DT:
                    exgT_war[ct % 2] = ttr
                xgT_t = xgT.tensor
                dst = bass.AP(xgT_t, xgT.offset + dt0 * C + ct * 128,
                              [[xgT_t.shape[1], 128], [C, TG], [1, 128]])
                tpx_war[j] = p.emit(
                    "vector", lambda e, i=tpx[j][:, 0:TG * 128].rearrange(
                        "p (g c) -> p g c", c=128), o=dst:
                    e.tensor_copy(o, i), wait=[("pe", ttr)], inc="dve")
        xgt_rdy[e_] = p.cnt["dve"]

    for e_ in range(E):
        u = SH + e_
        par = e_ % 2
        if e_ == 0:
            fetch(0)
        XGT = xgt_rdy[e_]
        a_stop = b_stop = None
        for ic in range(IC):
            slw = ic % 2
            la = p.dma("sync", s1a[:, slw * D:(slw + 1) * D], w1d[u, ic, :, :],
                       inc="da%d" % slw, wait=[("pe", s1a_war[slw])])
            lb = p.dma("scalar", s1b[:, slw * D:(slw + 1) * D], w3d[u, ic, :, :],
                       inc="db%d" % slw, wait=[("pe", s1b_war[slw])])
            for cq in range(NCQ):
                for dt in range(DT):
                    w = [("da%d" % slw, la), ("dve", XGT)]
                    if dt == 0 and cq == 0:
                        w.append(("act", easp_war))
                    a_stop = p.emit(
                        "tensor", lambda e, o=apse[:, cq * CQ:(cq + 1) * CQ],
                        l=s1a[:, slw * D + dt * 128: slw * D + (dt + 1) * 128],
                        r_=xgT[:, dt * C + cq * CQ: dt * C + (cq + 1) * CQ],
                        s0=(dt == 0), s1=(dt == DT - 1):
                        e.matmul(o, l, r_, start=s0, stop=s1), wait=w, inc="pe")
            s1a_war[slw] = a_stop
            for cq in range(NCQ):
                for dt in range(DT):
                    w = [("db%d" % slw, lb)]
                    if dt == 0 and cq == 0:
                        w.append(("dve", ebsp_war))
                    b_stop = p.emit(
                        "tensor", lambda e, o=bpse[:, cq * CQ:(cq + 1) * CQ],
                        l=s1b[:, slw * D + dt * 128: slw * D + (dt + 1) * 128],
                        r_=xgT[:, dt * C + cq * CQ: dt * C + (cq + 1) * CQ],
                        s0=(dt == 0), s1=(dt == DT - 1):
                        e.matmul(o, l, r_, start=s0, stop=s1), wait=w, inc="pe")
            s1b_war[slw] = b_stop
            easp_war = p.emit("scalar", lambda e: e.activation(silu[:, 0:C], apse[:, :],
                                                               AF.Silu),
                              wait=[("pe", a_stop), ("dve", ebsp_war),
                                    ("dsa0", eysc_war[0]),
                                    ("dsa1", eysc_war[1])], inc="act")
            ebsp_war = p.emit("vector", lambda e, o=ht[:, ic * C:(ic + 1) * C]:
                              e.tensor_mul(o, silu[:, 0:C], bpse[:, :]),
                              wait=[("act", easp_war), ("pe", b_stop),
                                    ("pe", eht_war)], inc="dve")
            ht_tgt[(u, ic)] = ebsp_war
        for ct in range(NCT):
            ev = {}
            for oh in range(NO2):
                sly = (ct * NO2 + oh) % 2
                mm_stop = None
                for n5 in range(N5):
                    for ic in range(IC):
                        w = [("dve", ht_tgt[(u, ic)]), ("dw2", w2_tgt[u])]
                        if ic == 0 and n5 == 0 and wy_war[sly] is not None:
                            w.append(wy_war[sly])
                        mm_stop = p.emit(
                            "tensor",
                            lambda e, o=yps[:, sly * OW2 + n5 * W5: sly * OW2 + n5 * W5 + W5],
                            l=ht[:, ic * C + ct * 128: ic * C + (ct + 1) * 128],
                            r_=w2sb[:, ic * D + oh * OW2 + n5 * W5:
                                    ic * D + oh * OW2 + n5 * W5 + W5],
                            s0=(ic == 0), s1=(ic == IC - 1):
                            e.matmul(o, l, r_, start=s0, stop=s1), wait=w, inc="pe")
                est2_last = mm_stop
                ev[oh] = p.emit(
                    "scalar",
                    lambda e, o=ysc[:, oh * OW2:(oh + 1) * OW2],
                    i=yps[:, sly * OW2:(sly + 1) * OW2],
                    sc=wtb[:, par * NCT + ct: par * NCT + ct + 1]:
                    e.activation(o, i, AF.Identity, scale=sc),
                    wait=[("pe", mm_stop), ("dsa%d" % par, eysc_war[par])],
                    inc="act")
                wy_war[sly] = ("act", ev[oh])
            dst = ymoeA if par == 0 else ymoeB
            if ct == 0:
                eht_war = est2_last
                easp_war = max(easp_war, max(ev.values()))
                if e_ + 1 < E:
                    fetch(e_ + 1)
            easp_war = max(easp_war, max(ev.values()))
            eysc_war[par] = p.emit(
                "gpsimd",
                lambda e, ix=ixb[:, par * NCT + ct: par * NCT + ct + 1], d=dst,
                evt=max(ev.values()):
                e.indirect_dma_start(d[:, :], IndirectOffsetOnAxis(ix, 0),
                                     ysc[:, :], None,
                                     bounds_check=rbT, oob_is_err=False,
                                     compute_op=ALU.add),
                wait=[("act", max(ev.values())), ("dz", ZMOE),
                      ("dsa%d" % par, eysc_war[par])],
                inc="dsa%d" % par, amt=16)
        eht_war = est2_last
        ixb_war[par] = eysc_war[par]
        if stop_after == "p7g" and e_ == 0:
            return early_out(y1[:, 0:D])
        if e_ + 1 < E:
            w2loads(u + 1, est2_last)

    SCAT_DONE = [("dsa0", p.cnt["dsa0"]), ("dsa1", p.cnt["dsa1"])]
    if stop_after == "p8y":
        return early_out(y1[:, 0:D])

    ymw = w2sb.bitcast(F32)
    L = max(1, min(4, IC // 2))
    ym_war = [0] * L
    wbase = [("pe", est2_last), ("dw2", p.cnt["dw2"])] + SCAT_DONE
    for tt in range(TT):
        prev = y_final[tt]
        for j, buf in enumerate((ymoeA, ymoeB)):
            s = (tt * 2 + j) % L
            ld = p.dma("sync", ymw[:, s * D:(s + 1) * D],
                       buf[tt * 128:(tt + 1) * 128, :], inc="dxy",
                       wait=wbase + [("dve", ym_war[s])])
            prev = p.emit("vector", lambda e, o=y1[:, tt * D:(tt + 1) * D],
                          i=ymw[:, s * D:(s + 1) * D]:
                          e.tensor_add(o, o, i),
                          wait=[("dxy", ld), ("dve", prev)], inc="dve")
            ym_war[s] = prev
        p.dma("sync", out_t[tt * 128:(tt + 1) * 128, :], y1[:, tt * D:(tt + 1) * D],
              inc="dout", wait=[("dve", prev)])
    p.emit("sync", lambda e: e.nop(), wait=[("dout", p.cnt["dout"])])

    st_ps7e.close()
    st_ps7y.close()
    st_P7.close()
    st_y1.close()

    with contextlib.ExitStack() as rs:
        sems = {}
        for name in p.cnt.keys():
            sems[name] = rs.enter_context(nc.semaphore("sem_" + name))
        block = rs.enter_context(nc.Block())
        p.replay(block, sems)
    st_persist.close()
    return nc


def _prep_inputs(cfg, x, wq, wk, wv, wo, gate_w, w1, w2, w3, sw1, sw2, sw3):
    T, D, HD, E, I, SI, R = (cfg[k] for k in ("T", "D", "HD", "E", "I", "SI", "R"))
    H, DT, IC, SH = D // HD, D // 128, I // 128, SI // I
    assert E >= 2
    NU = SH + E

    def t4(w):
        wt = np.ascontiguousarray(np.asarray(w).T)
        v = wt.reshape(DT, 128, DT, 128)
        return np.ascontiguousarray(v.transpose(2, 1, 0, 3).reshape(DT, 128, D))

    def tile1(wT):
        v = wT.reshape(DT, 128, IC, 128)
        return np.ascontiguousarray(v.transpose(2, 1, 0, 3).reshape(IC, 128, DT * 128))

    def tile2(w2T):
        return np.ascontiguousarray(np.asarray(w2T).reshape(IC, 128, D))

    w1u = np.empty((NU, IC, 128, DT * 128), dtype=BF)
    w3u = np.empty((NU, IC, 128, DT * 128), dtype=BF)
    w2u = np.empty((NU, IC, 128, D), dtype=BF)
    sw1 = np.asarray(sw1); sw2 = np.asarray(sw2); sw3 = np.asarray(sw3)
    for sh in range(SH):
        sl = slice(sh * I, (sh + 1) * I)
        w1u[sh] = tile1(np.ascontiguousarray(sw1[sl, :].T))
        w3u[sh] = tile1(np.ascontiguousarray(sw3[sl, :].T))
        w2u[sh] = tile2(np.ascontiguousarray(sw2[:, sl].T))
    for e in range(E):
        w1u[SH + e] = tile1(np.ascontiguousarray(np.asarray(w1)[e].T))
        w3u[SH + e] = tile1(np.ascontiguousarray(np.asarray(w3)[e].T))
        w2u[SH + e] = tile2(np.ascontiguousarray(np.asarray(w2)[e].T))

    gT = np.ascontiguousarray(np.asarray(gate_w).T)
    gate4 = np.ascontiguousarray(gT.reshape(DT, 128, E).transpose(1, 0, 2)
                                 .reshape(128, DT * E))

    C = min(T, max(2 * (2 * T // E), 128))
    TTh = T // 128
    tril = np.triu(np.ones((128, 128), np.float32)).astype(BF)
    tril8 = np.triu(np.ones((8, 8), np.float32), 1).astype(BF)
    onesc = np.ones((128, 1), dtype=BF)
    iotae = np.tile(np.arange(E, dtype=np.float32), (128, 1))
    bigf = np.full((128, E * C // 128), 1 << 20, dtype=np.int32)
    tokid = np.tile(np.arange(TTh, dtype=np.int32) * 128, (128, 1)) + \
        np.arange(128, dtype=np.int32)[:, None]
    ident = np.eye(128, dtype=BF)
    permM = np.zeros((128, 128), dtype=BF)
    for i in range(0, 128, 2):
        permM[i + 1, i] = 1.0
        permM[i, i + 1] = 1.0
    onehot = np.zeros((128, H * H), dtype=BF)
    for h in range(H):
        onehot[:, h * H + h] = 1.0
    ones1 = np.ones((1, 128), dtype=np.float32)

    scale = np.arange(0, HD, 2, dtype=np.float32) / HD
    inv_freq = (1.0 / 10000.0 ** scale).astype(np.float64)

    x = np.asarray(x).reshape(-1, D)
    S_total = x.shape[0]
    assert S_total == R * T
    in_maps = []
    common = dict(
        wq4=t4(wq).astype(np.float32), wk4=t4(wk).astype(np.float32),
        wv4=t4(wv).astype(np.float32),
        wot=np.ascontiguousarray(np.asarray(wo).T).astype(np.float32),
        gate4=gate4.astype(np.float32),
        w1t=w1u, w3t=w3u, w2t=w2u,
        ident=ident, perm=permM, onehot=onehot, ones1=ones1,
        tril=tril, tril8=tril8, onesc=onesc, iotae=iotae, bigf=bigf, tokid=tokid,
        cvals=np.tile(np.array(_CONST_VALS(cfg) + [0.0] * (16 - len(_CONST_VALS(cfg))),
                              dtype=np.float32), (128, 1)),
    )
    for c in range(R):
        pos = np.arange(c * T, (c + 1) * T, dtype=np.float64)
        ang = pos[None, :] * inv_freq[:, None]
        c2 = np.empty((128, T), dtype=np.float32)
        s2 = np.empty((128, T), dtype=np.float32)
        c2[0::2, :] = np.cos(ang); c2[1::2, :] = np.cos(ang)
        s2[0::2, :] = -np.sin(ang); s2[1::2, :] = np.sin(ang)
        m = dict(common)
        m["x"] = np.ascontiguousarray(x[c * T:(c + 1) * T, :]).astype(np.float32)
        m["c2t"] = c2
        m["s2t"] = s2
        in_maps.append(m)
    return in_maps


def kernel(L0_attn_x, L0_attn_wq, L0_attn_wk, L0_attn_wv, L0_attn_wo,
           L0_ffn_gate_w, L0_ffn_w1, L0_ffn_w2, L0_ffn_w3,
           L0_ffn_sw1, L0_ffn_sw2, L0_ffn_sw3):
    cfg = CFG_FULL
    in_maps = _prep_inputs(cfg, L0_attn_x, L0_attn_wq, L0_attn_wk, L0_attn_wv,
                           L0_attn_wo, L0_ffn_gate_w, L0_ffn_w1, L0_ffn_w2,
                           L0_ffn_w3, L0_ffn_sw1, L0_ffn_sw2, L0_ffn_sw3)
    nc = build_program(cfg)
    res = run_bass_kernel_spmd(nc, in_maps, core_ids=list(range(cfg["R"])))
    out = np.concatenate([np.asarray(r["out"]) for r in res.results], axis=0)
    return out.reshape(1, cfg["R"] * cfg["T"], cfg["D"]).astype(np.float32)


# version 19
# speedup vs baseline: 1.0791x; 1.0791x over previous
import math
import numpy as np
import ml_dtypes
import concourse.bass as bass
import concourse.mybir as mybir
from concourse.bass_utils import run_bass_kernel_spmd
from concourse.bass_core_rust import IndirectOffsetOnAxis

F32 = mybir.dt.float32
BF16 = mybir.dt.bfloat16
I32 = mybir.dt.int32
AF = mybir.ActivationFunctionType
ALU = mybir.AluOpType
AX = mybir.AxisListType
BF = ml_dtypes.bfloat16

CFG_FULL = dict(T=1024, D=2048, HD=128, E=16, I=1408, SI=2816, R=8)


def _CONST_VALS(cfg):
    return [1e-5, -0.5, 1.0 / math.sqrt(cfg["HD"]), 1.0, -1.0,
            1.0 / cfg["D"], 1e30, 0.0, 2.0, 0.5]


class Prog:
    ENGINES = ("tensor", "vector", "scalar", "gpsimd", "sync")

    def __init__(self, nc):
        self.nc = nc
        self.ops = {e: [] for e in self.ENGINES}
        self.cnt = {}
        self.bar_floor = 0

    def sem(self, *names):
        for n in names:
            self.cnt[n] = 0

    def emit(self, eng, fn, wait=(), inc=None, amt=1, nobar=False):
        w = [(s, v) for (s, v) in wait if v and v > 0]
        for s, v in w:
            assert v <= self.cnt[s], f"forward wait {s}:{v}>{self.cnt[s]} on {eng}"
        if not nobar and self.bar_floor:
            w.append(("bar", self.bar_floor))
        tgt = None
        if inc is not None:
            self.cnt[inc] += amt
            tgt = self.cnt[inc]
        self.ops[eng].append((fn, w, inc, amt))
        return tgt

    def dma(self, eng, out, in_, inc, wait=(), accum=False, slow_ok=False):
        def fn(e, o=out, i=in_, a=accum, s=slow_ok):
            kw = {}
            if a:
                kw["accum_op"] = mybir.AluOpType.add
            if s:
                kw["allow_slow_non_contiguous"] = True
            return e.dma_start(out=o, in_=i, **kw)
        w = list(wait)
        w.append((inc, self.cnt[inc]))
        return self.emit(eng, fn, wait=w, inc=inc, amt=16)

    def barrier(self):
        prev = self.bar_floor
        for e in self.ENGINES:
            if e == "sync":
                fn = lambda eng: eng.nop()
            else:
                fn = lambda eng: eng.drain()
            self.emit(e, fn, wait=([("bar", prev)] if prev else []), inc="bar",
                      nobar=True)
        self.bar_floor = self.cnt["bar"]

    def replay(self, block, sems):
        for name in self.ENGINES:
            ops = self.ops[name]

            def mk(ops=ops):
                def body(eng):
                    last = {}
                    for fn, waits, inc, amt in ops:
                        for s, v in waits:
                            if last.get(s, 0) < v:
                                eng.wait_ge(sems[s], v)
                                last[s] = v
                        ins = fn(eng)
                        if inc is not None:
                            ins.then_inc(sems[inc], amt)
                return body

            getattr(block, name)(mk())


def _treeplan(n):
    steps, cur, tmp = [], [("e", i) for i in range(n)], 0
    while len(cur) > 1:
        nxt = []
        for i in range(0, len(cur) - 1, 2):
            dst = ("s", 0) if len(cur) == 2 else ("t", tmp)
            steps.append((dst, cur[i], cur[i + 1]))
            nxt.append(dst)
            tmp += 1
        if len(cur) % 2:
            nxt.append(cur[-1])
        cur = nxt
    return steps


def build_program(cfg, stop_after=None):
    T, D, HD, E, I, SI, R = (cfg[k] for k in ("T", "D", "HD", "E", "I", "SI", "R"))
    H = D // HD
    TT = T // 128
    DT = D // 128
    IC = I // 128
    KT = T // 128
    SH = SI // I
    SHTT = None
    NU = SH + E
    Q5 = min(512, T)
    NQ = T // Q5
    RT = 1.0 / math.sqrt(HD)
    SHTT = TT
    assert SI == SH * I and H * HD == D

    nc = bass.Bass(trn_type="TRN2", target_bir_lowering=False)

    x_in = nc.dram_tensor("x", [T, D], F32, kind="ExternalInput")
    c2d = nc.dram_tensor("c2t", [128, T], F32, kind="ExternalInput")
    s2d = nc.dram_tensor("s2t", [128, T], F32, kind="ExternalInput")
    wq4 = nc.dram_tensor("wq4", [DT, 128, D], F32, kind="ExternalInput")
    wk4 = nc.dram_tensor("wk4", [DT, 128, D], F32, kind="ExternalInput")
    wv4 = nc.dram_tensor("wv4", [DT, 128, D], F32, kind="ExternalInput")
    wotd = nc.dram_tensor("wot", [D, D], F32, kind="ExternalInput")
    gate4 = nc.dram_tensor("gate4", [128, DT * E], F32, kind="ExternalInput")
    w1d = nc.dram_tensor("w1t", [NU, IC, 128, DT * 128], BF16, kind="ExternalInput")
    w3d = nc.dram_tensor("w3t", [NU, IC, 128, DT * 128], BF16, kind="ExternalInput")
    w2d = nc.dram_tensor("w2t", [NU, IC, 128, D], BF16, kind="ExternalInput")
    identd = nc.dram_tensor("ident", [128, 128], BF16, kind="ExternalInput")
    permd = nc.dram_tensor("perm", [128, 128], BF16, kind="ExternalInput")
    onehotd = nc.dram_tensor("onehot", [128, H * H], BF16, kind="ExternalInput")
    ones1d = nc.dram_tensor("ones1", [1, 128], F32, kind="ExternalInput")
    cvalsd = nc.dram_tensor("cvals", [128, 16], F32, kind="ExternalInput")
    out_t = nc.dram_tensor("out", [T, D], F32, kind="ExternalOutput")
    C = min(T, max(2 * (2 * T // E), 128))
    trild = nc.dram_tensor("tril", [128, 128], BF16, kind="ExternalInput")
    tril8d = nc.dram_tensor("tril8", [8, 8], BF16, kind="ExternalInput")
    onescd = nc.dram_tensor("onesc", [128, 1], BF16, kind="ExternalInput")
    iotaed = nc.dram_tensor("iotae", [128, E], F32, kind="ExternalInput")
    bigfd = nc.dram_tensor("bigf", [128, E * C // 128], I32, kind="ExternalInput")
    tokidd = nc.dram_tensor("tokid", [128, TT], I32, kind="ExternalInput")
    ln2_dram = nc.dram_tensor("ln2_dram", [T, D], BF16)
    idxtab = nc.dram_tensor("idxtab", [E * C, 1], I32)
    wtab = nc.dram_tensor("wtab", [E * C, 1], F32)
    ymoeA = nc.dram_tensor("ymoeA", [T, D], F32)
    ymoeB = nc.dram_tensor("ymoeB", [T, D], F32)
    kt_in = nc.dram_tensor("kt_in", [D, T], BF16)
    v_in = nc.dram_tensor("v_in", [T, D], BF16)
    aspace = "Shared" if R > 4 else "Local"
    kt_all = nc.dram_tensor("kt_all", [R * D, T], BF16, addr_space=aspace)
    v_all = nc.dram_tensor("v_all", [R * T, D], BF16, addr_space=aspace)

    p = Prog(nc)
    p.sem("bar", "pe", "act", "dve", "pool",
          "dc", "dx0", "dx1", "dxy", "gw0", "gw1", "gw2", "gwf",
          "dk0", "dk1", "dk2", "dk3", "dv0", "dv1",
          "dko0", "dko1", "dvs0", "dvs1",
          "da0", "da1", "db0", "db1", "dw2",
          "dl0", "dl1", "dit", "dgx", "dix", "dz", "dsa0", "dsa1",
          "dfl", "dout", "cc1", "cc2")

    import contextlib
    st_persist = contextlib.ExitStack()
    SP_ = lambda stk: (lambda *a, **k: stk.enter_context(nc.sbuf_tensor(*a, **k)))
    PP_ = lambda stk: (lambda *a, **k: stk.enter_context(nc.psum_tensor(*a, **k)))
    S0 = SP_(st_persist)

    ident = S0("c_ident", [128, 128], BF16)
    perm = S0("c_perm", [128, 128], BF16)
    onehot = S0("c_onehot", [128, H * H], BF16)
    ones1 = S0("c_ones1", [1, 128], F32)
    gate = S0("c_gate", [128, DT * E], BF16)
    stats = S0("stats", [128, 16], F32)
    sgate = S0("sgate", [128, TT * E], F32)
    dwall = S0("dwall", [128, TT * E], F32)
    tke = S0("tke", [128, 5 * E], F32)
    trils = S0("c_tril", [128, 128], BF16)
    tril8s = S0("c_tril8", [8, 8], BF16)
    onescs = S0("c_onesc", [128, 1], BF16)
    iotaes = S0("c_iotae", [128, E], F32)
    tokids = S0("c_tokid", [128, TT], I32)
    e1all = S0("e1all", [128, TT * E], F32)
    e2all = S0("e2all", [128, TT * E], F32)
    cvals = S0("c_cvals", [128, 16], F32)

    for dst, src in ((ident, identd), (perm, permd), (onehot, onehotd),
                     (trils, trild), (tril8s, tril8d), (onescs, onescd),
                     (iotaes, iotaed), (tokids, tokidd)):
        p.dma("sync", dst[:, :], src[:, :], inc="dc")
    p.dma("sync", ones1[0:1, :], ones1d[0:1, :], inc="dc")
    p.dma("sync", cvals[:, :], cvalsd[:, :], inc="dc")
    CONST = p.cnt["dc"]
    for ci, cv in enumerate(_CONST_VALS(cfg)):
        nc.const_aps.register(F32, cv, cvals[:, ci:ci + 1])
    for _e in Prog.ENGINES:
        p.emit(_e, lambda e: e.nop(), wait=[("dc", CONST)])
    p.dma("gpsimd", gate[:, :], gate4[:, :], inc="gwf")

    st_at = contextlib.ExitStack()
    at = st_at.enter_context(nc.sbuf_tensor("at", [128, H * T], BF16, side="right"))
    st_qt = contextlib.ExitStack()
    qt = st_qt.enter_context(nc.sbuf_tensor("qt", [128, H * T], BF16, side="right"))

    def mk_ln(lnb, dst_t, tpsq, ready, extra_war=None):
        if extra_war is None:
            extra_war = [[], []]
        lnb_war = [0, 0]
        tps_war = [0] * len(tpsq)
        stats_war = [0, 0]

        def emit_tile(tt, src, xw, done_fn=None):
            sl = tt % 2
            so = sl * 8
            v1 = p.emit("vector", lambda e, s=src, o=stats[:, so:so + 1]:
                        e.reduce_sum(o, s, axis=AX.X),
                        wait=xw + [("act", stats_war[sl])], inc="dve")
            a2 = p.emit("scalar", lambda e, s=src, o=lnb[:, sl * D:(sl + 1) * D],
                        a=stats[:, so + 1:so + 2]:
                        e.activation(o, s, AF.Square, accum_out=a),
                        wait=xw + [("pe", lnb_war[sl])] + extra_war[sl], inc="act")
            v = p.emit("vector", lambda e, i=stats[:, so:so + 1], o=stats[:, so + 2:so + 3]:
                       e.tensor_scalar_mul(o, i, 1.0 / D),
                       wait=[("act", a2), ("dve", v1)], inc="dve")
            v = p.emit("vector", lambda e, i=stats[:, so + 1:so + 2], o=stats[:, so + 3:so + 4]:
                       e.tensor_scalar_mul(o, i, 1.0 / D), wait=[("dve", v)], inc="dve")
            v = p.emit("vector", lambda e, m=stats[:, so + 2:so + 3], o=stats[:, so + 4:so + 5]:
                       e.tensor_mul(o, m, m), wait=[("dve", v)], inc="dve")
            vv = p.emit("vector", lambda e, a=stats[:, so + 3:so + 4],
                        b=stats[:, so + 4:so + 5], o=stats[:, so + 5:so + 6]:
                        e.tensor_sub(o, a, b), wait=[("dve", v)], inc="dve")
            al = p.emit("scalar", lambda e, i=stats[:, so + 5:so + 6], o=stats[:, so + 6:so + 7]:
                        e.activation(o, i, AF.Ln, bias=1e-5), wait=[("dve", vv)], inc="act")
            ar = p.emit("scalar", lambda e, i=stats[:, so + 6:so + 7], o=stats[:, so + 7:so + 8]:
                        e.activation(o, i, AF.Exp, scale=-0.5), wait=[("act", al)], inc="act")
            v = p.emit("vector", lambda e, m=stats[:, so + 2:so + 3], r=stats[:, so + 7:so + 8],
                       o=stats[:, so + 4:so + 5]:
                       e.tensor_mul(o, m, r), wait=[("act", ar), ("dve", vv)], inc="dve")
            vnb = p.emit("vector", lambda e, i=stats[:, so + 4:so + 5], o=stats[:, so + 4:so + 5]:
                         e.tensor_scalar_mul(o, i, -1.0), wait=[("dve", v)], inc="dve")
            alnb = p.emit("scalar",
                          lambda e, s=src, o=lnb[:, sl * D:(sl + 1) * D],
                          sc=stats[:, so + 7:so + 8], b=stats[:, so + 4:so + 5]:
                          e.activation(o, s, AF.Identity, scale=sc, bias=b),
                          wait=[("dve", vnb), ("act", ar), ("pe", lnb_war[sl])], inc="act")
            stats_war[sl] = alnb
            ready[("lnb", tt)] = alnb
            if done_fn is not None:
                done_fn(tt, [("act", alnb), ("dve", v1)])
            LG = 4 if DT % 4 == 0 else 1
            for dt0 in range(0, DT, LG):
                j = ((tt * DT + dt0) // LG) % 2
                ttr = None
                for q in range(LG):
                    dt = dt0 + q
                    ttr = p.emit(
                        "tensor",
                        lambda e, i=lnb[:, sl * D + dt * 128: sl * D + (dt + 1) * 128],
                        o=tpsq[j][:, q * 128:(q + 1) * 128]:
                        e.transpose(o, i, ident[:, :]),
                        wait=[("act", alnb), ("dve", tps_war[j]), ("dc", CONST)],
                        inc="pe")
                if dt0 + LG == DT:
                    lnb_war[sl] = ttr
                dt_h = dst_t
                dst = bass.AP(dt_h, dt0 * T + tt * 128,
                              [[dt_h.shape[1], 128], [T, LG], [1, 128]])
                tgt = p.emit(
                    "vector",
                    lambda e, i=tpsq[j][:, 0:LG * 128].rearrange(
                        "p (g c) -> p g c", c=128), o=dst:
                    e.tensor_copy(o, i), wait=[("pe", ttr)], inc="dve")
                tps_war[j] = tgt
                for q in range(LG):
                    ready[(dt0 + q, tt)] = tgt
        return emit_tile

    st_A = contextlib.ExitStack()
    SA = SP_(st_A)
    zt = SA("c_zt", [128, D], F32)
    ln1t = SA("ln1t", [128, DT * T], BF16)
    xb = SA("xb", [128, 2 * D], F32)
    lnb1 = SA("lnb1", [128, 2 * D], BF16)
    c2 = SA("c_c2", [128, T], F32)
    s2 = SA("c_s2", [128, T], F32)
    p.dma("sync", c2[:, :], c2d[:, :], inc="dc")
    p.dma("sync", s2[:, :], s2d[:, :], inc="dc")
    CS_TGT = p.cnt["dc"]
    st_ps0 = contextlib.ExitStack()
    tps0 = [PP_(st_ps0)("tps0a", [128, 512], BF16),
            PP_(st_ps0)("tps0b", [128, 512], BF16)]

    p.emit("vector", lambda e: e.memset(zt[:, :], 0.0), inc="dve")
    ZT_T = p.cnt["dve"]
    for tt in range(TT):
        p.dma("sync", ymoeA[tt * 128:(tt + 1) * 128, :], zt[:, :], inc="dz",
              wait=[("dve", ZT_T)])
        p.dma("sync", ymoeB[tt * 128:(tt + 1) * 128, :], zt[:, :], inc="dz",
              wait=[("dve", ZT_T)])
    p.dma("sync", wtab.ap().rearrange("(p n) a -> p (n a)", p=128),
          zt[:, 0:E * C // 128], inc="dz", wait=[("dve", ZT_T)])
    ZMOE = p.cnt["dz"]
    IDXF = p.dma("gpsimd", idxtab.ap().rearrange("(p n) a -> p (n a)", p=128),
                 bigfd[:, :], inc="dit")
    x_tgt = {}
    x_tgt[0] = p.dma("sync", xb[:, 0:D], x_in[0:128, :], inc="dx0")
    if TT > 1:
        x_tgt[1] = p.dma("sync", xb[:, D:2 * D], x_in[128:256, :], inc="dx1")

    def x_done(tt, tgts):
        if tt + 2 < TT:
            x_tgt[tt + 2] = p.dma("sync", xb[:, (tt % 2) * D:(tt % 2 + 1) * D],
                                  x_in[(tt + 2) * 128:(tt + 3) * 128, :],
                                  inc="dx%d" % (tt % 2), wait=tgts)

    ln1_ready = {}
    ln1_tile = mk_ln(lnb1, ln1t, tps0, ln1_ready)
    for tt in range(TT):
        ln1_tile(tt, xb[:, (tt % 2) * D:(tt % 2 + 1) * D],
                 [("dx%d" % (tt % 2), x_tgt[tt])], x_done)
    LN1_ALL = max(ln1_ready.values())
    p.barrier()
    st_ps0.close()

    def early_out(src_sb):
        p.barrier()
        eng = "sync" if src_sb.tensor.dtype == F32 else "gpsimd"
        for tt in range(TT):
            p.dma(eng, out_t[tt * 128:(tt + 1) * 128, :], src_sb,
                  inc="dout", wait=[])
        p.emit("sync", lambda e: e.nop(), wait=[("dout", p.cnt["dout"])])
        with contextlib.ExitStack() as rs:
            sems = {}
            for name in p.cnt.keys():
                sems[name] = rs.enter_context(nc.semaphore("sem_" + name))
            block = rs.enter_context(nc.Block())
            p.replay(block, sems)
        return nc

    if stop_after == "p0":
        return early_out(ln1t[:, 0:D])

    st_P1 = contextlib.ExitStack()
    S1 = SP_(st_P1)
    wb = S1("wb", [128, 3 * D], BF16)
    qe = S1("qe", [128, 3 * T], BF16)
    t1b = S1("t1b", [128, T], F32)
    t2b = S1("t2b", [128, T], F32)
    kst = S1("kst", [128, 2 * T], BF16)
    vst = S1("vst", [128, 2 * T], BF16)
    st_ps1 = contextlib.ExitStack()
    pps = PP_(st_ps1)("pps", [128, 2 * T], F32)
    st_ps1qk = contextlib.ExitStack()
    sps = PP_(st_ps1qk)("sps", [128, 2 * T], F32)

    wb_war = [0, 0, 0]
    pps_war = [[], []]
    sps_war = [[], []]
    qe_war = [[], [], []]
    t12_war = [0]
    kst_war = [0, 0]
    vst_war = [0, 0]
    vtb_war = [0, 0, 0]

    def proj(kind, w4, limit=None):
        for ot in range(DT):
            sl = ot % 2
            sl3 = ot % 3
            ld = p.dma("gpsimd", wb[:, sl3 * D:(sl3 + 1) * D], w4[ot, :, :],
                       inc="gw%d" % sl3, wait=[("pe", wb_war[sl3])])
            mm_stop = None
            for qn in range(NQ):
                for dt in range(DT):
                    w = [("gw%d" % sl3, ld), ("dve", LN1_ALL)]
                    if dt == 0 and qn == 0:
                        w += pps_war[sl]
                    mm_stop = p.emit(
                        "tensor",
                        lambda e, o=pps[:, sl * T + qn * Q5: sl * T + qn * Q5 + Q5],
                        l=wb[:, sl3 * D + dt * 128: sl3 * D + (dt + 1) * 128],
                        r=ln1t[:, dt * T + qn * Q5: dt * T + qn * Q5 + Q5],
                        s0=(dt == 0), s1=(dt == DT - 1):
                        e.matmul(o, l, r, start=s0, stop=s1),
                        wait=w, inc="pe")
            wb_war[sl3] = mm_stop
            pslice = pps[:, sl * T:(sl + 1) * T]
            if kind in ("q", "k"):
                aev = p.emit("scalar", lambda e, i=pslice, o=qe[:, sl3 * T:(sl3 + 1) * T]:
                             e.activation(o, i, AF.Copy),
                             wait=[("pe", mm_stop)] + qe_war[sl3], inc="act")
                if limit == "mm":
                    pps_war[sl] = [("act", aev)]
                    qe_war[sl3] = [("pe", mm_stop)]
                    continue
                sw_stop = None
                for qn in range(NQ):
                    w = [("act", aev)]
                    if qn == 0:
                        w += sps_war[sl]
                    sw_stop = p.emit(
                        "tensor",
                        lambda e, o=sps[:, sl * T + qn * Q5: sl * T + qn * Q5 + Q5],
                        r=qe[:, sl3 * T + qn * Q5: sl3 * T + qn * Q5 + Q5]:
                        e.matmul(o, perm[:, :], r, start=True, stop=True),
                        wait=w, inc="pe")
                qe_war[sl3] = [("pe", sw_stop)]
                if limit == "swap":
                    pps_war[sl] = [("act", aev)]
                    sps_war[sl] = []
                    continue
                v1 = p.emit("vector", lambda e, i=qe[:, sl3 * T:(sl3 + 1) * T]:
                            e.tensor_mul(t1b[:, :], i, c2[:, :]),
                            wait=[("act", aev), ("dve", t12_war[0]),
                                  ("dc", CS_TGT)], inc="dve")
                v2 = p.emit("vector", lambda e, i=sps[:, sl * T:(sl + 1) * T]:
                            e.tensor_mul(t2b[:, :], i, s2[:, :]),
                            wait=[("pe", sw_stop), ("dve", t12_war[0])], inc="dve")
                pps_war[sl] = [("act", aev)]
                sps_war[sl] = [("dve", v2)]
                qe_war[sl3] = [("pe", sw_stop), ("dve", v1)]
                if limit == "rope1":
                    t12_war[0] = v2
                    continue
                if kind == "q":
                    dst = qt[:, ot * T:(ot + 1) * T]
                    vo = p.emit("vector", lambda e, d=dst:
                                e.tensor_add(d, t1b[:, :], t2b[:, :]),
                                wait=[("dve", v1), ("dve", v2)], inc="dve")
                    t12_war[0] = vo
                else:
                    dst = kst[:, sl * T:(sl + 1) * T]
                    vo = p.emit("vector", lambda e, d=dst:
                                e.tensor_add(d, t1b[:, :], t2b[:, :]),
                                wait=[("dve", v1), ("dve", v2),
                                      ("dko%d" % sl, kst_war[sl])], inc="dve")
                    t12_war[0] = vo
                    kst_war[sl] = p.dma("sync", kt_in[ot * 128:(ot + 1) * 128, :],
                                        dst, inc="dko%d" % sl, wait=[("dve", vo)])
            else:
                aev = p.emit("scalar", lambda e, i=pslice, o=qe[:, sl3 * T:(sl3 + 1) * T]:
                             e.activation(o, i, AF.Copy),
                             wait=[("pe", mm_stop)] + qe_war[sl3], inc="act")
                pps_war[sl] = [("act", aev)]
                last_cp = None
                for tc in range(TT):
                    j = (ot * TT + tc) % 3
                    ttr = p.emit("tensor",
                                 lambda e, i=qe[:, sl3 * T + tc * 128: sl3 * T + (tc + 1) * 128],
                                 o=vtp[j][:, :]:
                                 e.transpose(o, i, ident[:, :]),
                                 wait=[("act", aev), ("dve", vtb_war[j])], inc="pe")
                    if tc == TT - 1:
                        qe_war[sl3] = [("pe", ttr)]
                    w = [("pe", ttr)]
                    if tc == 0:
                        w.append(("dvs%d" % sl, vst_war[sl]))
                    last_cp = p.emit("vector",
                                     lambda e, i=vtp[j][:, :],
                                     o=vst[:, sl * T + tc * 128: sl * T + (tc + 1) * 128]:
                                     e.tensor_copy(o, i), wait=w, inc="dve")
                    vtb_war[j] = last_cp
                dst_ap = v_in[:, ot * 128:(ot + 1) * 128].rearrange(
                    "(tc p) hd -> p tc hd", p=128)
                src_ap = vst[:, sl * T:(sl + 1) * T].rearrange(
                    "p (tc hd) -> p tc hd", hd=128)
                vst_war[sl] = p.dma("sync", dst_ap, src_ap,
                                    inc="dvs%d" % sl, wait=[("dve", last_cp)])

    if stop_after in ("p1q_mm", "p1q_swap", "p1q_rope1"):
        proj("q", wq4, limit=stop_after[4:])
        return early_out(qe[:, 0:D])
    proj("q", wq4)
    if stop_after == "p1q":
        return early_out(qt[:, 0:D])
    proj("k", wk4)
    k_wait = [(s, p.cnt[s]) for s in ("dko0", "dko1")]
    cc1t = p.emit("gpsimd", lambda e: e.collective_compute(
        "AllGather", ALU.bypass, replica_groups=[list(range(R))],
        ins=[kt_in.ap().opt()], outs=[kt_all.ap().opt()]),
        wait=k_wait, inc="cc1", amt=1)
    p.barrier()
    st_ps1qk.close()
    st_ps1v = contextlib.ExitStack()
    vtp = [PP_(st_ps1v)("vtpa", [128, 128], BF16),
           PP_(st_ps1v)("vtpb", [128, 128], BF16),
           PP_(st_ps1v)("vtpc", [128, 128], BF16)]
    proj("v", wv4)
    if stop_after == "p1":
        return early_out(qt[:, 0:D])
    v_wait = [(s, p.cnt[s]) for s in ("dvs0", "dvs1")]
    cc2t = p.emit("gpsimd", lambda e: e.collective_compute(
        "AllGather", ALU.bypass, replica_groups=[list(range(R))],
        ins=[v_in.ap().opt()], outs=[v_all.ap().opt()]),
        wait=v_wait, inc="cc2", amt=1)

    if stop_after == "cc":
        return early_out(qt[:, 0:D])
    p.emit("sync", lambda e: e.nop(),
           wait=[("dz", p.cnt["dz"]), ("dko0", p.cnt["dko0"]),
                 ("dko1", p.cnt["dko1"]), ("dvs0", p.cnt["dvs0"]),
                 ("dvs1", p.cnt["dvs1"])])
    p.barrier()
    st_ps1v.close()
    st_ps1.close()
    st_P1.close()
    st_A.close()

    st_P3 = contextlib.ExitStack()
    S3 = SP_(st_P3)
    ktb = S3("ktb", [128, 4 * T], BF16)
    vb = S3("vb", [128, 2 * KT * D], BF16)
    expt = S3("expt", [128, 2 * KT * T], BF16)
    tplan = _treeplan(KT)
    ntmp = sum(1 for d, _, _ in tplan if d[0] == "t")
    tbuf = S3("tbuf", [128, max(1, ntmp) * T], BF16)
    sch = S3("sch", [128, 2 * T], BF16)
    st_ps3 = contextlib.ExitStack()
    scp = PP_(st_ps3)("scp", [128, 2 * T], F32)
    pvp = PP_(st_ps3)("pvp", [128, T], F32)
    psS = PP_(st_ps3)("psS", [H, T], F32)

    ktb_war = [0, 0, 0, 0]
    vb_war = [0, 0]
    kt_tgt = {}
    v_tgt = {}

    def load_kt(m):
        r, h = divmod(m, H)
        kt_tgt[m] = p.dma("sync", ktb[:, (m % 4) * T:(m % 4 + 1) * T],
                          kt_all[r * D + h * 128: r * D + (h + 1) * 128, :],
                          inc="dk%d" % (m % 4),
                          wait=[("pe", ktb_war[m % 4]), ("cc1", 1)])

    def load_v(r):
        src = v_all[r * T:(r + 1) * T, :].rearrange("(kt p) d -> p kt d", p=128)
        dst = vb[:, (r % 2) * KT * D:(r % 2 + 1) * KT * D].rearrange(
            "p (kt d) -> p kt d", d=D)
        v_tgt[r] = p.dma("sync", dst, src,
                         inc="dv%d" % (r % 2),
                         wait=[("pe", vb_war[r % 2]), ("cc2", 1)])

    load_v(0)
    for m0 in range(min(4, R * H)):
        load_kt(m0)

    scp_war = [0, 0]
    pvp_war = []
    expt_war = [[], []]
    sch_war = [0, 0]
    smm_last = 0
    at_last = [0] * H
    pend_dsum = None
    tree_last = [0]
    ndsum = [0]

    def emit_dsum(mm_, hh, wlist):
        nonlocal smm_last
        for qn in range(NQ):
            smm_last = p.emit(
                "tensor",
                lambda e, o=psS[0:H, qn * Q5:(qn + 1) * Q5],
                l=onehot[:, hh * H:(hh + 1) * H],
                r_=sch[:, (mm_ % 2) * T + qn * Q5:(mm_ % 2) * T + qn * Q5 + Q5],
                s0=(ndsum[0] == 0), s1=(ndsum[0] == R * H - 1):
                e.matmul(o, l, r_, start=s0, stop=s1),
                wait=wlist, inc="pe")
        ndsum[0] += 1
        sch_war[mm_ % 2] = smm_last

    for m in range(R * H):
        r, h = divmod(m, H)
        s2_ = m % 2
        if h == 0 and r + 1 < R:
            load_v(r + 1)
        ex_t = {}
        pv_stop = None

        def emit_sc(kt):
            j = kt % 2
            mm = None
            for qn in range(NQ):
                w = [("dk%d" % (m % 4), kt_tgt[m]), ("act", scp_war[j])]
                mm = p.emit("tensor",
                            lambda e, o=scp[:, j * T + qn * Q5: j * T + qn * Q5 + Q5],
                            l=ktb[:, (m % 4) * T + kt * 128:(m % 4) * T + (kt + 1) * 128],
                            r_=qt[:, h * T + qn * Q5: h * T + qn * Q5 + Q5]:
                            e.matmul(o, l, r_, start=True, stop=True),
                            wait=w, inc="pe")
            w2 = [("pe", mm)]
            if kt == 0:
                w2 += expt_war[s2_]
            ex_t[kt] = p.emit(
                "scalar",
                lambda e, i=scp[:, j * T:(j + 1) * T],
                o=expt[:, s2_ * KT * T + kt * T: s2_ * KT * T + (kt + 1) * T]:
                e.activation(o, i, AF.Exp, scale=RT), wait=w2, inc="act")
            scp_war[j] = ex_t[kt]

        def emit_pv(kt):
            nonlocal pv_stop
            for qn in range(NQ):
                w = [("act", ex_t[kt]), ("dv%d" % (r % 2), v_tgt[r])]
                if kt == 0 and qn == 0:
                    w += pvp_war
                pv_stop = p.emit(
                    "tensor",
                    lambda e, o=pvp[:, qn * Q5: qn * Q5 + Q5],
                    l=vb[:, (r % 2) * KT * D + kt * D + h * 128:
                          (r % 2) * KT * D + kt * D + (h + 1) * 128],
                    r_=expt[:, s2_ * KT * T + kt * T + qn * Q5:
                            s2_ * KT * T + kt * T + qn * Q5 + Q5],
                    s0=(kt == 0), s1=(kt == KT - 1):
                    e.matmul(o, l, r_, start=s0, stop=s1),
                    wait=w, inc="pe")

        emit_sc(0)
        if KT > 1:
            emit_sc(1)
        for kt in range(KT):
            if kt + 2 < KT:
                emit_sc(kt + 2)
            emit_pv(kt)
        if pend_dsum is not None:
            emit_dsum(*pend_dsum)
            pend_dsum = None
        ktb_war[m % 4] = pv_stop
        if h == H - 1:
            vb_war[r % 2] = pv_stop
        node_t = {}
        lastv = None
        lastp = 0
        first_step = True
        for si, (dst, a, b) in enumerate(tplan):
            def ap_of(n):
                k, i = n
                if k == "e":
                    return expt[:, s2_ * KT * T + i * T:(s2_ * KT * T + (i + 1) * T)]
                if k == "t":
                    return tbuf[:, i * T:(i + 1) * T]
                return sch[:, (m % 2) * T:(m % 2 + 1) * T]
            eng = "vector"
            w = []
            for n in (a, b):
                if n[0] == "e":
                    w.append(("act", ex_t[n[1]]))
                elif n in node_t:
                    w.append(node_t[n])
            if dst[0] == "s":
                w.append(("pe", sch_war[m % 2]))
            if first_step:
                w.append(("dve", tree_last[0]))
                first_step = False
            tgt = p.emit(eng, lambda e, o=ap_of(dst), x=ap_of(a), y=ap_of(b):
                         e.tensor_add(o, x, y), wait=w, inc="dve")
            node_t[dst] = ("dve", tgt)
            lastv = tgt
            tree_last[0] = tgt
        pend_dsum = (m, h, [node_t[("s", 0)]])
        ew = [("pe", pv_stop)]
        if lastv:
            ew.append(("dve", lastv))
        expt_war[s2_] = ew
        if r == 0:
            vac = p.emit("vector", lambda e, i=pvp[:, :],
                         o=at[:, h * T:(h + 1) * T]:
                         e.tensor_copy(o, i), wait=[("pe", pv_stop)], inc="dve")
        else:
            vac = p.emit("vector", lambda e, i=pvp[:, :],
                         o=at[:, h * T:(h + 1) * T]:
                         e.tensor_add(o, o, i),
                         wait=[("pe", pv_stop), ("dve", at_last[h])], inc="dve")
        at_last[h] = vac
        pvp_war = [("dve", vac)]
        if m + 4 < R * H:
            load_kt(m + 4)

    if stop_after == "p3":
        return early_out(at[:, 0:D])

    if pend_dsum is not None:
        emit_dsum(*pend_dsum)

    p.barrier()
    st_P3.close()
    st_P4 = contextlib.ExitStack()
    S4 = SP_(st_P4)
    sums = S4("sums", [H, T], F32)
    rsums = S4("rsums", [H, T], F32)
    rsflat = S4("rsflat", [1, H * T], F32)
    vsc = p.emit("vector", lambda e: e.tensor_copy(sums[0:H, :], psS[0:H, :]),
                 wait=[("pe", smm_last)], inc="dve")
    vrc = p.emit("vector", lambda e: e.reciprocal(rsums[0:H, :], sums[0:H, :]),
                 wait=[("dve", vsc)], inc="dve")
    p.barrier()
    st_ps3.close()
    st_ps4 = contextlib.ExitStack()
    rps = PP_(st_ps4)("rps", [128, 2 * T], F32)
    at_sc_h = {}
    fl = p.dma("sync", rsflat[0:1, :].rearrange("a (h t) -> a h t", h=H),
               rsums[0:H, :],
               inc="dfl", wait=[("dve", vrc)])
    rps_war = [0, 0]
    for h in range(H):
        r_stop = None
        for qn in range(NQ):
            w = [("dfl", fl)]
            if qn == 0:
                w.append(("dve", rps_war[h % 2]))
            r_stop = p.emit("tensor",
                            lambda e, o=rps[:, (h % 2) * T + qn * Q5:
                                            (h % 2) * T + qn * Q5 + Q5],
                            r_=rsflat[0:1, h * T + qn * Q5: h * T + qn * Q5 + Q5]:
                            e.matmul(o, ones1[0:1, :], r_, start=True, stop=True),
                            wait=w, inc="pe")
        rps_war[h % 2] = p.emit("vector",
                                lambda e, o=at[:, h * T:(h + 1) * T],
                                i=rps[:, (h % 2) * T:(h % 2 + 1) * T]:
                                e.tensor_mul(o, o, i),
                                wait=[("pe", r_stop), ("dve", at_last[h])], inc="dve")
        at_last[h] = rps_war[h % 2]
        at_sc_h[h] = rps_war[h % 2]
    AT_SCALED = p.cnt["dve"]
    p.emit("sync", lambda e: e.nop(), wait=[("dfl", p.cnt["dfl"])])
    p.barrier()
    st_ps4.close()
    st_P4.close()
    st_qt.close()

    if stop_after == "p4":
        return early_out(at[:, 0:D])

    st_y1 = contextlib.ExitStack()
    y1 = SP_(st_y1)("y1", [128, TT * D], F32)
    ln2t = SP_(st_y1)("ln2t", [128, DT * T], BF16)
    for tt in range(TT):
        p.dma("scalar", y1[:, tt * D:(tt + 1) * D],
              x_in[tt * 128:(tt + 1) * 128, :], inc="dxy")
    XY_ALL = p.cnt["dxy"]
    st_P5 = contextlib.ExitStack()
    wo_sb = SP_(st_P5)("wo_sb", [128, DT * D], BF16)
    lnb2 = SP_(st_P5)("lnb2", [128, 2 * D], BF16)
    OH = 2 if D >= 1024 else 1
    OW = D // OH
    O5 = min(512, OW)
    NO5 = OW // O5
    st_ps5 = contextlib.ExitStack()
    wop = PP_(st_ps5)("wop", [128, 2 * OW], F32)
    tps6 = [PP_(st_ps5)("tps6a", [128, 512], BF16),
            PP_(st_ps5)("tps6b", [128, 512], BF16)]
    gps = [PP_(st_ps5)("gpsa", [128, E], F32),
           PP_(st_ps5)("gpsb", [128, E], F32)]

    for oh in range(OH):
        for dt in range(DT):
            p.dma("gpsimd", wo_sb[:, dt * D + oh * OW: dt * D + (oh + 1) * OW],
                  wotd[dt * 128:(dt + 1) * 128, oh * OW:(oh + 1) * OW], inc="gwf")
    WO_ALL = p.cnt["gwf"]
    y1_attn = {}
    wop_war = [0, 0]
    ln2_ready = {}
    ln2_xwar = [[], []]
    ln2_tile = mk_ln(lnb2, ln2t, tps6, ln2_ready, extra_war=ln2_xwar)
    gps_war = [0, 0]
    dw_tgt = {}
    tke_war = [0]

    def wo_tile(tt):
        for oh in range(OH):
            sl = (tt * OH + oh) % 2
            mm_stop = None
            for n5 in range(NO5):
                for dt in range(DT):
                    w = [("gwf", WO_ALL), ("dve", at_sc_h[dt])]
                    if dt == 0 and n5 == 0:
                        w += [("dve", wop_war[sl])]
                    mm_stop = p.emit(
                        "tensor",
                        lambda e, o=wop[:, sl * OW + n5 * O5: sl * OW + n5 * O5 + O5],
                        l=at[:, dt * T + tt * 128: dt * T + (tt + 1) * 128],
                        r_=wo_sb[:, dt * D + oh * OW + n5 * O5:
                                 dt * D + oh * OW + n5 * O5 + O5],
                        s0=(dt == 0), s1=(dt == DT - 1):
                        e.matmul(o, l, r_, start=s0, stop=s1),
                        wait=w, inc="pe")
            wop_war[sl] = p.emit(
                "vector",
                lambda e, o=y1[:, tt * D + oh * OW: tt * D + (oh + 1) * OW],
                i=wop[:, sl * OW:(sl + 1) * OW]:
                e.tensor_add(o, o, i),
                wait=[("pe", mm_stop), ("dxy", XY_ALL)], inc="dve")
        y1_attn[tt] = wop_war[(tt * OH + OH - 1) % 2]

    def gate_tile(tt):
        sl = tt % 2
        g_stop = None
        for dt in range(DT):
            w = [("dve", ln2_ready[(dt, tt)]), ("gwf", p.cnt["gwf"])]
            if dt == 0:
                w.append(("act", gps_war[sl]))
            g_stop = p.emit("tensor",
                            lambda e, o=gps[sl][:, :],
                            l=ln2t[:, dt * T + tt * 128: dt * T + (tt + 1) * 128],
                            r_=gate[:, dt * E:(dt + 1) * E],
                            s0=(dt == 0), s1=(dt == DT - 1):
                            e.matmul(o, l, r_, start=s0, stop=s1),
                            wait=w, inc="pe")
        asg = p.emit("scalar", lambda e, i=gps[sl][:, :],
                     o=sgate[:, tt * E:(tt + 1) * E]:
                     e.activation(o, i, AF.Sigmoid),
                     wait=[("pe", g_stop)], inc="act")
        gps_war[sl] = asg
        sg = sgate[:, tt * E:(tt + 1) * E]
        v = p.emit("vector", lambda e, i=sg: e.reduce_max(tke[:, 4 * E:4 * E + 1], i, axis=AX.X),
                   wait=[("act", asg), ("dve", tke_war[0])], inc="dve")
        v = p.emit("vector", lambda e, i=sg: e.tensor_scalar(
            tke[:, 0:E], i, tke[:, 4 * E:4 * E + 1], None, ALU.is_ge),
            wait=[("dve", v)], inc="dve")
        v = p.emit("vector", lambda e: e.tensor_scalar(
            tke[:, E:2 * E], tke[:, 0:E], 1e30, None, ALU.mult),
            wait=[("dve", v)], inc="dve")
        v = p.emit("vector", lambda e, i=sg: e.tensor_sub(
            tke[:, 2 * E:3 * E], i, tke[:, E:2 * E]), wait=[("dve", v)], inc="dve")
        v = p.emit("vector", lambda e: e.reduce_max(
            tke[:, 4 * E + 1:4 * E + 2], tke[:, 2 * E:3 * E], axis=AX.X),
            wait=[("dve", v)], inc="dve")
        v = p.emit("vector", lambda e: e.tensor_scalar(
            tke[:, E:2 * E], tke[:, 2 * E:3 * E], tke[:, 4 * E + 1:4 * E + 2],
            None, ALU.is_ge), wait=[("dve", v)], inc="dve")
        v = p.emit("vector", lambda e, o=e1all[:, tt * E:(tt + 1) * E]:
                   e.tensor_copy(o, tke[:, 0:E]), wait=[("dve", v)], inc="dve")
        v = p.emit("vector", lambda e, o=e2all[:, tt * E:(tt + 1) * E]:
                   e.tensor_copy(o, tke[:, E:2 * E]), wait=[("dve", v)], inc="dve")
        v = p.emit("vector", lambda e: e.tensor_add(
            tke[:, 3 * E:4 * E], tke[:, 0:E], tke[:, E:2 * E]),
            wait=[("dve", v)], inc="dve")
        v = p.emit("vector", lambda e, i=sg: e.tensor_mul(
            tke[:, 0:E], tke[:, 3 * E:4 * E], i), wait=[("dve", v)], inc="dve")
        v = p.emit("vector", lambda e: e.tensor_add(
            tke[:, 4 * E + 2:4 * E + 3], tke[:, 4 * E:4 * E + 1],
            tke[:, 4 * E + 1:4 * E + 2]), wait=[("dve", v)], inc="dve")
        v = p.emit("vector", lambda e: e.reciprocal(
            tke[:, 4 * E + 3:4 * E + 4], tke[:, 4 * E + 2:4 * E + 3]),
            wait=[("dve", v)], inc="dve")
        dw_tgt[tt] = p.emit("vector", lambda e, o=dwall[:, tt * E:(tt + 1) * E]:
                            e.tensor_scalar(o, tke[:, 0:E], tke[:, 4 * E + 3:4 * E + 4],
                                            None, ALU.mult), wait=[("dve", v)], inc="dve")
        tke_war[0] = dw_tgt[tt]

    for tt in range(TT):
        wo_tile(tt)
        ln2_tile(tt, y1[:, tt * D:(tt + 1) * D], [("dve", y1_attn[tt])])
        sl_ = tt % 2
        dl = p.dma("sync", ln2_dram[tt * 128:(tt + 1) * 128, :],
                   lnb2[:, sl_ * D:(sl_ + 1) * D], inc="dl%d" % sl_,
                   wait=[("act", ln2_ready[("lnb", tt)])])
        ln2_xwar[sl_][:] = [("dl%d" % sl_, dl)]
        gate_tile(tt)
    LN2D = [("dl0", p.cnt["dl0"]), ("dl1", p.cnt["dl1"])]

    p.emit("sync", lambda e: e.nop(),
           wait=[("dl0", p.cnt["dl0"]), ("dl1", p.cnt["dl1"])])
    p.barrier()
    st_ps5.close()
    st_P5.close()
    st_at.close()

    if stop_after == "p5" or stop_after == "p6":
        return early_out(y1[:, 0:D])

    st_P7 = contextlib.ExitStack()
    S7 = SP_(st_P7)
    rbT = nc.gpsimd.alloc_register("rbT")
    rbEC = nc.gpsimd.alloc_register("rbEC")
    p.emit("gpsimd", lambda e: e.reg_mov(rbT, T - 1))
    p.emit("gpsimd", lambda e: e.reg_mov(rbEC, E * C - 1))
    ht = S7("ht", [128, IC * T], BF16)
    s1a = S7("s1a", [128, 2 * D], BF16)
    s1b = S7("s1b", [128, 2 * D], BF16)
    w2sb = S7("w2sb", [128, IC * D], BF16)
    silu = S7("silu", [128, D], F32)
    ysc = silu

    ixb = S7("ixb", [128, 2 * max(2, C // 128)], I32)
    wtb = S7("wtb", [128, 2 * max(2, C // 128)], F32)
    pfsb = S7("pfsb", [128, TT * E], F32)
    possb = pfsb
    cntsb = S7("cntsb", [128, TT * E], BF16)
    mkb = S7("mkb", [128, TT * E], BF16)
    basef = S7("basef", [128, 64 + TT * E], F32)
    ixw = S7("ixw", [128, 4], I32)
    wvw = S7("wvw", [128, 4], F32)
    OW2 = min(1024, D)
    NO2 = D // OW2
    W5 = min(512, OW2)
    N5 = OW2 // W5
    st_ps7y = contextlib.ExitStack()
    yps = PP_(st_ps7y)("yps", [128, 2 * OW2], F32)
    st_ps7a = contextlib.ExitStack()
    aps = PP_(st_ps7a)("aps", [128, T], F32)
    bps = PP_(st_ps7a)("bps", [128, T], F32)

    s1a_war = [0, 0]
    s1b_war = [0, 0]
    aps_war = 0
    bps_war = 0
    ht_war = {}
    ht_tgt = {}
    w2_tgt = {}
    wy_war = [None, None]
    y_final = dict((tt, 0) for tt in range(TT))
    stage2_last_mm = 0

    def w2loads(u, wait_pe):
        for ic in range(IC):
            p.dma("gpsimd", w2sb[:, ic * D:(ic + 1) * D],
                  w2d[u, ic, :, :], inc="dw2",
                  wait=([("pe", wait_pe)] if ic == 0 else []))
        w2_tgt[u] = p.cnt["dw2"]

    def stage1_chunk(u, ic):
        nonlocal aps_war, bps_war
        sl = ic % 2
        la = p.dma("sync", s1a[:, sl * D:(sl + 1) * D], w1d[u, ic, :, :],
                   inc="da%d" % sl, wait=[("pe", s1a_war[sl])])
        lb = p.dma("scalar", s1b[:, sl * D:(sl + 1) * D], w3d[u, ic, :, :],
                   inc="db%d" % sl, wait=[("pe", s1b_war[sl])])
        a_stop = b_stop = None
        for qn in range(NQ):
            for dt in range(DT):
                w = [("da%d" % sl, la), ("dve", ln2_ready[(dt, TT - 1)])]
                if dt == 0 and qn == 0:
                    w.append(("act", aps_war))
                a_stop = p.emit(
                    "tensor",
                    lambda e, o=aps[:, qn * Q5: qn * Q5 + Q5],
                    l=s1a[:, sl * D + dt * 128: sl * D + (dt + 1) * 128],
                    r_=ln2t[:, dt * T + qn * Q5: dt * T + qn * Q5 + Q5],
                    s0=(dt == 0), s1=(dt == DT - 1):
                    e.matmul(o, l, r_, start=s0, stop=s1), wait=w, inc="pe")
        s1a_war[sl] = a_stop
        for qn in range(NQ):
            for dt in range(DT):
                w = [("db%d" % sl, lb)]
                if dt == 0 and qn == 0:
                    w.append(("dve", bps_war))
                b_stop = p.emit(
                    "tensor",
                    lambda e, o=bps[:, qn * Q5: qn * Q5 + Q5],
                    l=s1b[:, sl * D + dt * 128: sl * D + (dt + 1) * 128],
                    r_=ln2t[:, dt * T + qn * Q5: dt * T + qn * Q5 + Q5],
                    s0=(dt == 0), s1=(dt == DT - 1):
                    e.matmul(o, l, r_, start=s0, stop=s1), wait=w, inc="pe")
        s1b_war[sl] = b_stop
        aps_war = p.emit("scalar", lambda e: e.activation(silu[:, 0:T], aps[:, :], AF.Silu),
                         wait=[("pe", a_stop), ("dve", bps_war)], inc="act")
        bps_war = p.emit("vector", lambda e, o=ht[:, ic * T:(ic + 1) * T]:
                         e.tensor_mul(o, silu[:, 0:T], bps[:, :]),
                         wait=[("act", aps_war), ("pe", b_stop),
                               ("pe", ht_war.get(u - 1, 0))], inc="dve")
        ht_tgt[(u, ic)] = bps_war

    def stage2_chunk(u, tt, oh, sl):
        nonlocal stage2_last_mm
        mm_stop = None
        for n5 in range(N5):
            for ic in range(IC):
                w = [("dve", ht_tgt[(u, ic)]), ("dw2", w2_tgt[u])]
                if ic == 0 and n5 == 0 and wy_war[sl] is not None:
                    w.append(wy_war[sl])
                mm_stop = p.emit(
                    "tensor",
                    lambda e, o=yps[:, sl * OW2 + n5 * W5: sl * OW2 + n5 * W5 + W5],
                    l=ht[:, ic * T + tt * 128: ic * T + (tt + 1) * 128],
                    r_=w2sb[:, ic * D + oh * OW2 + n5 * W5:
                            ic * D + oh * OW2 + n5 * W5 + W5],
                    s0=(ic == 0), s1=(ic == IC - 1):
                    e.matmul(o, l, r_, start=s0, stop=s1), wait=w, inc="pe")
        stage2_last_mm = mm_stop
        ysl = y1[:, tt * D + oh * OW2: tt * D + (oh + 1) * OW2]
        t_ = p.emit("vector",
                    lambda e, o=ysl, i=yps[:, sl * OW2:(sl + 1) * OW2]:
                    e.tensor_add(o, o, i),
                    wait=[("pe", mm_stop), ("dve", y_final[tt])], inc="dve")
        wy_war[sl] = ("dve", t_)
        y_final[tt] = max(y_final[tt], t_)

    w2loads(0, 0)
    for ic in range(IC):
        stage1_chunk(0, ic)

    pA = yps[:, 0:128]
    pB = yps[:, OW2:OW2 + 128]
    NE = TT * E
    MSK = ("dve", dw_tgt[TT - 1])
    v = p.emit("vector", lambda e: e.tensor_add(mkb[:, :], e1all[:, :], e2all[:, :]),
               wait=[MSK], inc="dve")
    mpf = p.emit("tensor", lambda e: e.matmul(pA[:, 0:NE], trils[:, :], mkb[:, 0:NE],
                                              start=True, stop=True),
                 wait=[("dve", v), ("dc", CONST)], inc="pe")
    mcn = p.emit("tensor", lambda e: e.matmul(pB[0:1, 0:NE], onescs[:, 0:1], mkb[:, 0:NE],
                                              start=True, stop=True),
                 wait=[("dve", v)], inc="pe")
    vpf = p.emit("vector", lambda e: e.tensor_copy(pfsb[:, :], pA[:, 0:NE]),
                 wait=[("pe", mpf)], inc="dve")
    acn = p.emit("scalar", lambda e: e.activation(cntsb[0:1, 0:NE], pB[0:1, 0:NE], AF.Copy),
                 wait=[("pe", mcn)], inc="act")
    fl1 = p.dma("sync", mkb[0:TT, 0:E],
                cntsb[0:1, 0:NE].rearrange("a (t e) -> a t e", t=TT),
                inc="dfl", wait=[("act", acn), ("pe", mpf), ("pe", mcn)])
    mbs = p.emit("tensor", lambda e: e.matmul(pB[0:SHTT, 0:E], tril8s[0:SHTT, 0:SHTT],
                                              mkb[0:SHTT, 0:E],
                                              start=True, stop=True),
                 wait=[("dfl", fl1), ("dve", vpf)], inc="pe")
    abs_ = p.emit("scalar", lambda e: e.activation(basef[0:SHTT, 0:E], pB[0:SHTT, 0:E],
                                                   AF.Copy),
                  wait=[("pe", mbs)], inc="act")
    fl2 = p.dma("sync", basef[0:1, 64:64 + NE].rearrange("a (t e) -> a t e", t=TT),
                basef[0:SHTT, 0:E],
                inc="dfl", wait=[("act", abs_)])
    mbb = p.emit("tensor", lambda e: e.matmul(pA[:, 0:NE], ones1[0:1, :],
                                              basef[0:1, 64:64 + NE],
                                              start=True, stop=True),
                 wait=[("dfl", fl2), ("dve", vpf)], inc="pe")
    v = p.emit("vector", lambda e: e.tensor_add(pfsb[:, :], pfsb[:, :], pA[:, 0:NE]),
               wait=[("pe", mbb), ("dve", vpf)], inc="dve")
    vpos = p.emit("vector", lambda e: e.tensor_scalar(pfsb[:, :], pfsb[:, :], -1.0,
                                                      None, ALU.add),
                  wait=[("dve", v)], inc="dve")
    wy_war[0] = ("dve", vpos)
    wy_war[1] = ("dve", vpos)

    sc_prev = [IDXF, IDXF, IDXF, IDXF]
    for tt in range(TT):
        for k in (1, 2):
            msk = (e1all if k == 1 else e2all)[:, tt * E:(tt + 1) * E]
            slk = (tt * 2 + (k - 1)) % 4
            v = p.emit("vector", lambda e, m=msk, ps=possb[:, tt * E:(tt + 1) * E]:
                       e.tensor_mul(tke[:, 0:E], ps, m),
                       wait=[("dve", vpos), ("dit", sc_prev[slk])], inc="dve")
            v = p.emit("vector", lambda e: e.reduce_sum(tke[:, 4 * E:4 * E + 1],
                                                        tke[:, 0:E], axis=AX.X),
                       wait=[("dve", v)], inc="dve")
            v = p.emit("vector", lambda e, m=msk: e.tensor_mul(tke[:, E:2 * E],
                                                               iotaes[:, :], m),
                       wait=[("dve", v)], inc="dve")
            v = p.emit("vector", lambda e: e.reduce_sum(tke[:, 4 * E + 1:4 * E + 2],
                                                        tke[:, E:2 * E], axis=AX.X),
                       wait=[("dve", v)], inc="dve")
            v = p.emit("vector", lambda e: e.tensor_scalar(
                tke[:, 4 * E + 2:4 * E + 3], tke[:, 4 * E + 1:4 * E + 2], float(C),
                None, ALU.mult), wait=[("dve", v)], inc="dve")
            v = p.emit("vector", lambda e: e.tensor_add(
                tke[:, 4 * E + 2:4 * E + 3], tke[:, 4 * E + 2:4 * E + 3],
                tke[:, 4 * E:4 * E + 1]), wait=[("dve", v)], inc="dve")
            v = p.emit("vector", lambda e, o=ixw[:, slk:slk + 1]:
                       e.tensor_copy(o, tke[:, 4 * E + 2:4 * E + 3]),
                       wait=[("dve", v)], inc="dve")
            v = p.emit("vector", lambda e, m=msk, dw=dwall[:, tt * E:(tt + 1) * E]:
                       e.tensor_mul(tke[:, 2 * E:3 * E], dw, m),
                       wait=[("dve", v)], inc="dve")
            vw = p.emit("vector", lambda e, o=wvw[:, slk:slk + 1]:
                        e.reduce_sum(o, tke[:, 2 * E:3 * E], axis=AX.X),
                        wait=[("dve", v)], inc="dve")
            s1_ = p.emit("gpsimd", lambda e, ix=ixw[:, slk:slk + 1], s=tokids[:, tt:tt + 1]:
                         e.indirect_dma_start(
                             idxtab[:, :], IndirectOffsetOnAxis(ix, 0), s, None,
                             bounds_check=rbEC, oob_is_err=False),
                         wait=[("dve", vw), ("dc", CONST),
                               ("dit", p.cnt["dit"])],
                         inc="dit", amt=16)
            sc_prev[slk] = p.emit("gpsimd", lambda e, ix=ixw[:, slk:slk + 1],
                                  s=wvw[:, slk:slk + 1]:
                                  e.indirect_dma_start(
                                      wtab[:, :], IndirectOffsetOnAxis(ix, 0), s, None,
                                      bounds_check=rbEC, oob_is_err=False),
                                  wait=[("dz", ZMOE), ("dit", p.cnt["dit"])],
                                  inc="dit", amt=16)
    IDX_DONE = p.cnt["dit"]

    sl = 0
    for u in range(SH):
        if u > 0:
            for ic in range(IC):
                stage1_chunk(u, ic)
        for tt in range(TT):
            for oh in range(NO2):
                stage2_chunk(u, tt, oh, sl % 2)
                sl += 1
        ht_war[u] = stage2_last_mm
        w2loads(u + 1, stage2_last_mm)

    if stop_after == "p7s":
        return early_out(y1[:, 0:D])

    p.barrier()
    st_ps7a.close()
    st_ps7e = contextlib.ExitStack()
    apse = PP_(st_ps7e)("apse", [128, C], F32)
    bpse = PP_(st_ps7e)("bpse", [128, C], F32)
    tpx = [PP_(st_ps7e)("tpxa", [128, 512], BF16),
           PP_(st_ps7e)("tpxb", [128, 512], BF16)]
    TG = 4 if DT % 4 == 0 else 1
    NCT = C // 128
    HTC = IC * C
    if HTC + DT * C + 2 * D <= IC * T:
        xgT = ht[:, HTC: HTC + DT * C]
        xg_slots = [ht[:, HTC + DT * C + s * D: HTC + DT * C + (s + 1) * D]
                    for s in range(2)]
    else:
        _xgt = S7("xgt_sb", [128, DT * C], BF16)
        _xgs = S7("xg_sb", [128, 2 * D], BF16)
        xgT = _xgt[:, :]
        xg_slots = [_xgs[:, s * D:(s + 1) * D] for s in range(2)]
    CQ = min(512, C)
    NCQ = C // CQ
    eht_war = stage2_last_mm
    exgT_war = [stage2_last_mm, stage2_last_mm]

    easp_war = 0
    ebsp_war = 0
    eysc_war = [0, 0]
    tpx_war = [0, 0]
    ixb_war = [0, 0]
    est2_last = stage2_last_mm

    xgt_rdy = {}

    def fetch(e_):
        par = e_ % 2
        p.dma("sync", ixb[:, par * NCT:(par + 1) * NCT],
              idxtab[e_ * C:(e_ + 1) * C, 0:1].rearrange("(ct p) a -> p (ct a)", p=128),
              inc="dix", wait=[("dit", IDX_DONE), ("dsa%d" % par, ixb_war[par])],
              slow_ok=True)
        p.dma("sync", wtb[:, par * NCT:(par + 1) * NCT],
              wtab[e_ * C:(e_ + 1) * C, 0:1].rearrange("(ct p) a -> p (ct a)", p=128),
              inc="dix", slow_ok=True)
        DIX = p.cnt["dix"]
        g_tgt = {}
        for ct in range(NCT):
            g_tgt[ct] = p.emit(
                "gpsimd",
                lambda e, ix=ixb[:, par * NCT + ct: par * NCT + ct + 1],
                o=xg_slots[ct % 2]:
                e.indirect_dma_start(o, None, ln2_dram[:, :],
                                     IndirectOffsetOnAxis(ix, 0),
                                     bounds_check=rbT, oob_is_err=False),
                wait=[("dix", DIX), ("pe", exgT_war[ct % 2]),
                      ("dgx", p.cnt["dgx"])] + LN2D,
                inc="dgx", amt=16)
        for ct in range(NCT):
            for dt0 in range(0, DT, TG):
                j = ((ct * DT + dt0) // TG) % 2
                ttr = None
                for q in range(TG):
                    dt = dt0 + q
                    ttr = p.emit("tensor",
                                 lambda e, i=xg_slots[ct % 2][:, dt * 128:(dt + 1) * 128],
                                 o=tpx[j][:, q * 128:(q + 1) * 128]:
                                 e.transpose(o, i, ident[:, :]),
                                 wait=[("dgx", g_tgt[ct]), ("dve", tpx_war[j]),
                                       ("pe", est2_last)], inc="pe")
                if dt0 + TG == DT:
                    exgT_war[ct % 2] = ttr
                xgT_t = xgT.tensor
                dst = bass.AP(xgT_t, xgT.offset + dt0 * C + ct * 128,
                              [[xgT_t.shape[1], 128], [C, TG], [1, 128]])
                tpx_war[j] = p.emit(
                    "vector", lambda e, i=tpx[j][:, 0:TG * 128].rearrange(
                        "p (g c) -> p g c", c=128), o=dst:
                    e.tensor_copy(o, i), wait=[("pe", ttr)], inc="dve")
        xgt_rdy[e_] = p.cnt["dve"]

    for e_ in range(E):
        u = SH + e_
        par = e_ % 2
        if e_ == 0:
            fetch(0)
        XGT = xgt_rdy[e_]
        a_stop = b_stop = None
        for ic in range(IC):
            slw = ic % 2
            la = p.dma("sync", s1a[:, slw * D:(slw + 1) * D], w1d[u, ic, :, :],
                       inc="da%d" % slw, wait=[("pe", s1a_war[slw])])
            lb = p.dma("scalar", s1b[:, slw * D:(slw + 1) * D], w3d[u, ic, :, :],
                       inc="db%d" % slw, wait=[("pe", s1b_war[slw])])
            for cq in range(NCQ):
                for dt in range(DT):
                    w = [("da%d" % slw, la), ("dve", XGT)]
                    if dt == 0 and cq == 0:
                        w.append(("act", easp_war))
                    a_stop = p.emit(
                        "tensor", lambda e, o=apse[:, cq * CQ:(cq + 1) * CQ],
                        l=s1a[:, slw * D + dt * 128: slw * D + (dt + 1) * 128],
                        r_=xgT[:, dt * C + cq * CQ: dt * C + (cq + 1) * CQ],
                        s0=(dt == 0), s1=(dt == DT - 1):
                        e.matmul(o, l, r_, start=s0, stop=s1), wait=w, inc="pe")
            s1a_war[slw] = a_stop
            for cq in range(NCQ):
                for dt in range(DT):
                    w = [("db%d" % slw, lb)]
                    if dt == 0 and cq == 0:
                        w.append(("dve", ebsp_war))
                    b_stop = p.emit(
                        "tensor", lambda e, o=bpse[:, cq * CQ:(cq + 1) * CQ],
                        l=s1b[:, slw * D + dt * 128: slw * D + (dt + 1) * 128],
                        r_=xgT[:, dt * C + cq * CQ: dt * C + (cq + 1) * CQ],
                        s0=(dt == 0), s1=(dt == DT - 1):
                        e.matmul(o, l, r_, start=s0, stop=s1), wait=w, inc="pe")
            s1b_war[slw] = b_stop
            easp_war = p.emit("scalar", lambda e: e.activation(silu[:, 0:C], apse[:, :],
                                                               AF.Silu),
                              wait=[("pe", a_stop), ("dve", ebsp_war),
                                    ("dsa0", eysc_war[0]),
                                    ("dsa1", eysc_war[1])], inc="act")
            ebsp_war = p.emit("vector", lambda e, o=ht[:, ic * C:(ic + 1) * C]:
                              e.tensor_mul(o, silu[:, 0:C], bpse[:, :]),
                              wait=[("act", easp_war), ("pe", b_stop),
                                    ("pe", eht_war)], inc="dve")
            ht_tgt[(u, ic)] = ebsp_war
        for ct in range(NCT):
            ev = {}
            for oh in range(NO2):
                sly = (ct * NO2 + oh) % 2
                mm_stop = None
                for n5 in range(N5):
                    for ic in range(IC):
                        w = [("dve", ht_tgt[(u, ic)]), ("dw2", w2_tgt[u])]
                        if ic == 0 and n5 == 0 and wy_war[sly] is not None:
                            w.append(wy_war[sly])
                        mm_stop = p.emit(
                            "tensor",
                            lambda e, o=yps[:, sly * OW2 + n5 * W5: sly * OW2 + n5 * W5 + W5],
                            l=ht[:, ic * C + ct * 128: ic * C + (ct + 1) * 128],
                            r_=w2sb[:, ic * D + oh * OW2 + n5 * W5:
                                    ic * D + oh * OW2 + n5 * W5 + W5],
                            s0=(ic == 0), s1=(ic == IC - 1):
                            e.matmul(o, l, r_, start=s0, stop=s1), wait=w, inc="pe")
                est2_last = mm_stop
                ev[oh] = p.emit(
                    "scalar",
                    lambda e, o=ysc[:, oh * OW2:(oh + 1) * OW2],
                    i=yps[:, sly * OW2:(sly + 1) * OW2],
                    sc=wtb[:, par * NCT + ct: par * NCT + ct + 1]:
                    e.activation(o, i, AF.Identity, scale=sc),
                    wait=[("pe", mm_stop), ("dsa%d" % par, eysc_war[par])],
                    inc="act")
                wy_war[sly] = ("act", ev[oh])
            dst = ymoeA if par == 0 else ymoeB
            if ct == 0:
                eht_war = est2_last
                easp_war = max(easp_war, max(ev.values()))
                if e_ + 1 < E:
                    fetch(e_ + 1)
            easp_war = max(easp_war, max(ev.values()))
            eysc_war[par] = p.emit(
                "gpsimd",
                lambda e, ix=ixb[:, par * NCT + ct: par * NCT + ct + 1], d=dst,
                evt=max(ev.values()):
                e.indirect_dma_start(d[:, :], IndirectOffsetOnAxis(ix, 0),
                                     ysc[:, :], None,
                                     bounds_check=rbT, oob_is_err=False,
                                     compute_op=ALU.add),
                wait=[("act", max(ev.values())), ("dz", ZMOE),
                      ("dsa%d" % par, eysc_war[par])],
                inc="dsa%d" % par, amt=16)
        eht_war = est2_last
        ixb_war[par] = eysc_war[par]
        if stop_after == "p7g" and e_ == 0:
            return early_out(y1[:, 0:D])
        if e_ + 1 < E:
            w2loads(u + 1, est2_last)

    SCAT_DONE = [("dsa0", p.cnt["dsa0"]), ("dsa1", p.cnt["dsa1"])]
    if stop_after == "p8y":
        return early_out(y1[:, 0:D])

    ymw = w2sb.bitcast(F32)
    L = max(1, min(4, IC // 2))
    ym_war = [0] * L
    wbase = [("pe", est2_last), ("dw2", p.cnt["dw2"])] + SCAT_DONE
    for tt in range(TT):
        prev = y_final[tt]
        for j, buf in enumerate((ymoeA, ymoeB)):
            s = (tt * 2 + j) % L
            ld = p.dma("sync", ymw[:, s * D:(s + 1) * D],
                       buf[tt * 128:(tt + 1) * 128, :], inc="dxy",
                       wait=wbase + [("dve", ym_war[s])])
            prev = p.emit("vector", lambda e, o=y1[:, tt * D:(tt + 1) * D],
                          i=ymw[:, s * D:(s + 1) * D]:
                          e.tensor_add(o, o, i),
                          wait=[("dxy", ld), ("dve", prev)], inc="dve")
            ym_war[s] = prev
        p.dma("sync", out_t[tt * 128:(tt + 1) * 128, :], y1[:, tt * D:(tt + 1) * D],
              inc="dout", wait=[("dve", prev)])
    p.emit("sync", lambda e: e.nop(), wait=[("dout", p.cnt["dout"])])

    st_ps7e.close()
    st_ps7y.close()
    st_P7.close()
    st_y1.close()

    with contextlib.ExitStack() as rs:
        sems = {}
        for name in p.cnt.keys():
            sems[name] = rs.enter_context(nc.semaphore("sem_" + name))
        block = rs.enter_context(nc.Block())
        p.replay(block, sems)
    st_persist.close()
    return nc


def _prep_inputs(cfg, x, wq, wk, wv, wo, gate_w, w1, w2, w3, sw1, sw2, sw3):
    T, D, HD, E, I, SI, R = (cfg[k] for k in ("T", "D", "HD", "E", "I", "SI", "R"))
    H, DT, IC, SH = D // HD, D // 128, I // 128, SI // I
    assert E >= 2
    NU = SH + E

    def t4(w):
        wt = np.ascontiguousarray(np.asarray(w).T)
        v = wt.reshape(DT, 128, DT, 128)
        return np.ascontiguousarray(v.transpose(2, 1, 0, 3).reshape(DT, 128, D))

    def tile1(wT):
        v = wT.reshape(DT, 128, IC, 128)
        return np.ascontiguousarray(v.transpose(2, 1, 0, 3).reshape(IC, 128, DT * 128))

    def tile2(w2T):
        return np.ascontiguousarray(np.asarray(w2T).reshape(IC, 128, D))

    w1u = np.empty((NU, IC, 128, DT * 128), dtype=BF)
    w3u = np.empty((NU, IC, 128, DT * 128), dtype=BF)
    w2u = np.empty((NU, IC, 128, D), dtype=BF)
    sw1 = np.asarray(sw1); sw2 = np.asarray(sw2); sw3 = np.asarray(sw3)
    for sh in range(SH):
        sl = slice(sh * I, (sh + 1) * I)
        w1u[sh] = tile1(np.ascontiguousarray(sw1[sl, :].T))
        w3u[sh] = tile1(np.ascontiguousarray(sw3[sl, :].T))
        w2u[sh] = tile2(np.ascontiguousarray(sw2[:, sl].T))
    for e in range(E):
        w1u[SH + e] = tile1(np.ascontiguousarray(np.asarray(w1)[e].T))
        w3u[SH + e] = tile1(np.ascontiguousarray(np.asarray(w3)[e].T))
        w2u[SH + e] = tile2(np.ascontiguousarray(np.asarray(w2)[e].T))

    gT = np.ascontiguousarray(np.asarray(gate_w).T)
    gate4 = np.ascontiguousarray(gT.reshape(DT, 128, E).transpose(1, 0, 2)
                                 .reshape(128, DT * E))

    C = min(T, max(2 * (2 * T // E), 128))
    TTh = T // 128
    tril = np.triu(np.ones((128, 128), np.float32)).astype(BF)
    tril8 = np.triu(np.ones((8, 8), np.float32), 1).astype(BF)
    onesc = np.ones((128, 1), dtype=BF)
    iotae = np.tile(np.arange(E, dtype=np.float32), (128, 1))
    bigf = np.full((128, E * C // 128), 1 << 20, dtype=np.int32)
    tokid = np.tile(np.arange(TTh, dtype=np.int32) * 128, (128, 1)) + \
        np.arange(128, dtype=np.int32)[:, None]
    ident = np.eye(128, dtype=BF)
    permM = np.zeros((128, 128), dtype=BF)
    for i in range(0, 128, 2):
        permM[i + 1, i] = 1.0
        permM[i, i + 1] = 1.0
    onehot = np.zeros((128, H * H), dtype=BF)
    for h in range(H):
        onehot[:, h * H + h] = 1.0
    ones1 = np.ones((1, 128), dtype=np.float32)

    scale = np.arange(0, HD, 2, dtype=np.float32) / HD
    inv_freq = (1.0 / 10000.0 ** scale).astype(np.float64)

    x = np.asarray(x).reshape(-1, D)
    S_total = x.shape[0]
    assert S_total == R * T
    in_maps = []
    common = dict(
        wq4=t4(wq).astype(np.float32), wk4=t4(wk).astype(np.float32),
        wv4=t4(wv).astype(np.float32),
        wot=np.ascontiguousarray(np.asarray(wo).T).astype(np.float32),
        gate4=gate4.astype(np.float32),
        w1t=w1u, w3t=w3u, w2t=w2u,
        ident=ident, perm=permM, onehot=onehot, ones1=ones1,
        tril=tril, tril8=tril8, onesc=onesc, iotae=iotae, bigf=bigf, tokid=tokid,
        cvals=np.tile(np.array(_CONST_VALS(cfg) + [0.0] * (16 - len(_CONST_VALS(cfg))),
                              dtype=np.float32), (128, 1)),
    )
    for c in range(R):
        pos = np.arange(c * T, (c + 1) * T, dtype=np.float64)
        ang = pos[None, :] * inv_freq[:, None]
        c2 = np.empty((128, T), dtype=np.float32)
        s2 = np.empty((128, T), dtype=np.float32)
        c2[0::2, :] = np.cos(ang); c2[1::2, :] = np.cos(ang)
        s2[0::2, :] = -np.sin(ang); s2[1::2, :] = np.sin(ang)
        m = dict(common)
        m["x"] = np.ascontiguousarray(x[c * T:(c + 1) * T, :]).astype(np.float32)
        m["c2t"] = c2
        m["s2t"] = s2
        in_maps.append(m)
    return in_maps


def kernel(L0_attn_x, L0_attn_wq, L0_attn_wk, L0_attn_wv, L0_attn_wo,
           L0_ffn_gate_w, L0_ffn_w1, L0_ffn_w2, L0_ffn_w3,
           L0_ffn_sw1, L0_ffn_sw2, L0_ffn_sw3):
    cfg = CFG_FULL
    in_maps = _prep_inputs(cfg, L0_attn_x, L0_attn_wq, L0_attn_wk, L0_attn_wv,
                           L0_attn_wo, L0_ffn_gate_w, L0_ffn_w1, L0_ffn_w2,
                           L0_ffn_w3, L0_ffn_sw1, L0_ffn_sw2, L0_ffn_sw3)
    nc = build_program(cfg)
    res = run_bass_kernel_spmd(nc, in_maps, core_ids=list(range(cfg["R"])))
    out = np.concatenate([np.asarray(r["out"]) for r in res.results], axis=0)
    return out.reshape(1, cfg["R"] * cfg["T"], cfg["D"]).astype(np.float32)
